# version 37
# baseline (speedup 1.0000x reference)
import numpy as np
import ml_dtypes
import contextlib
import concourse.bass as bass
import concourse.mybir as mybir
from concourse.bass_utils import run_bass_kernel_spmd

F32 = mybir.dt.float32
BF16 = mybir.dt.bfloat16
AF = mybir.ActivationFunctionType
ALU = mybir.AluOpType
BF = ml_dtypes.bfloat16

D = 1024
E = 2048
N = 16
FF = 4096
L = 4096
T = 256
NT = L // T
Q = 128
DC = D // 128
EC = E // 128
FC = FF // 128

ENGINES = ("sync", "gpsimd", "vector", "scalar", "tensor")
NPOOL = 8
DPOOL = {"sy": ["d_sy%d" % i for i in range(NPOOL)],
         "gp": ["d_gp%d" % i for i in range(NPOOL)]}
BARSEMS = tuple("b%d_%s" % (k, e) for k in range(4) for e in ("sy", "gp", "ve", "sc", "te"))
SEMS = ("s_sy", "s_gp", "s_ve", "s_sc", "s_te", "tick") + tuple(DPOOL["sy"]) \
    + tuple(DPOOL["gp"]) + BARSEMS
ESEM = {"sync": "s_sy", "gpsimd": "s_gp", "vector": "s_ve",
        "scalar": "s_sc", "tensor": "s_te"}


class Sch:

    def __init__(self):
        self.ops = []
        self.dma_ct = {"sy": 0, "gp": 0}

    def op(self, eng, fn, deps=(), force_sem=None):
        rec = {"eng": eng, "fn": fn, "dma": False, "idx": len(self.ops),
               "force_sem": force_sem,
               "deps": [d for d in deps if d is not None]}
        self.ops.append(rec)
        return rec

    def dma(self, queue, fn, deps=()):
        eng = "sync" if queue == "sy" else "gpsimd"
        sem = DPOOL[queue][self.dma_ct[queue] % NPOOL]
        self.dma_ct[queue] += 1
        rec = {"eng": eng, "fn": fn, "dma": True, "q": sem,
               "idx": len(self.ops), "deps": [d for d in deps if d is not None]}
        self.ops.append(rec)
        return rec

    def assign(self):
        self.waited = waited = set()
        for rec in self.ops:
            for d in rec["deps"]:
                if d["dma"] or d["eng"] != rec["eng"]:
                    waited.add(d["idx"])
        cum = {s: 0 for s in SEMS}
        for rec in self.ops:
            if rec["dma"]:
                rec["sem"] = rec["q"]
                rec["inc"] = 16
            elif rec.get("force_sem"):
                rec["sem"] = rec["force_sem"]
                rec["inc"] = 1
            else:
                rec["sem"] = ESEM[rec["eng"]]
                rec["inc"] = 1 if rec["idx"] in waited else 0
            cum[rec["sem"]] += rec["inc"]
            rec["val"] = cum[rec["sem"]]
        run = {s: 0 for s in SEMS}
        for rec in self.ops:
            waits = {}
            for d in rec["deps"]:
                if not d["dma"] and d["eng"] == rec["eng"]:
                    continue
                s = d["sem"]
                v = run[s] if d["dma"] else d["val"]
                assert d["inc"] > 0
                if waits.get(s, 0) < v:
                    waits[s] = v
            rec["waits"] = sorted(waits.items())
            if rec["dma"]:
                run[rec["sem"]] += rec["inc"]
        return cum


ENG2SHORT = {"sync": "sy", "gpsimd": "gp", "vector": "ve", "scalar": "sc",
             "tensor": "te"}


def all_barrier(S, k, extra_deps=None):
    extra_deps = extra_deps or {}
    b1 = {e: S.op(e, lambda eng: eng.nop(), deps=extra_deps.get(e, []),
                  force_sem="b%d_%s" % (k, ENG2SHORT[e])) for e in ENGINES}
    b2 = {e: S.op(e, lambda eng: eng.nop(),
                  deps=[b1[x] for x in ENGINES if x != e],
                  force_sem="b%d_%s" % (k + 1, ENG2SHORT[e])) for e in ENGINES}
    return b1, b2


def build_nc():
    nc = bass.Bass("TRN2", target_bir_lowering=False,
                   detect_race_conditions=False)

    d_x = nc.dram_tensor("xT", [128, DC * L], F32, kind="ExternalInput").ap()
    d_win = nc.dram_tensor("WinT", [128, DC * 4096], F32, kind="ExternalInput").ap()
    d_wout = nc.dram_tensor("WoutT", [128, EC * 1024], F32, kind="ExternalInput").ap()
    d_wxd = nc.dram_tensor("Wxd", [128, EC * 48], F32, kind="ExternalInput").ap()
    d_wfc = nc.dram_tensor("WfcT", [128, DC * 4096], F32, kind="ExternalInput").ap()
    d_wproj = nc.dram_tensor("WprojT", [128, FC * 1024], F32, kind="ExternalInput").ap()
    d_convw = nc.dram_tensor("convw", [128, EC * 4], F32, kind="ExternalInput").ap()
    d_convb = nc.dram_tensor("convb", [128, EC], F32, kind="ExternalInput").ap()
    d_dvec = nc.dram_tensor("dvec", [128, EC], F32, kind="ExternalInput").ap()
    d_bdt = nc.dram_tensor("bdt", [16, 1], F32, kind="ExternalInput").ap()
    d_alog = nc.dram_tensor("alog", [16, 1], F32, kind="ExternalInput").ap()
    d_identf = nc.dram_tensor("identf", [128, 128], F32, kind="ExternalInput").ap()
    d_identb = nc.dram_tensor("identb", [128, 128], BF16, kind="ExternalInput").ap()
    d_tri = nc.dram_tensor("tri", [128, 128], F32, kind="ExternalInput").ap()
    d_onesb = nc.dram_tensor("onesb", [128, 1], BF16, kind="ExternalInput").ap()
    d_mneg = nc.dram_tensor("mneg", [128, 128], BF16, kind="ExternalInput").ap()

    d_out = nc.dram_tensor("outT", [128, DC * L], F32, kind="ExternalOutput").ap()
    d_xn = nc.dram_tensor("xn_hbm", [128, DC * L], BF16, kind="Internal").ap()
    d_attn = nc.dram_tensor("attn_hbm", [128, DC * L], BF16, kind="Internal").ap()

    S = Sch()
    ctx = contextlib.ExitStack()
    A = ctx.enter_context

    identf = A(nc.sbuf_tensor("identf_s", [128, 128], F32))
    identb = A(nc.sbuf_tensor("identb_s", [128, 128], BF16))
    onesb = A(nc.sbuf_tensor("onesb_s", [128, 1], BF16))
    mneg = A(nc.sbuf_tensor("mneg_s", [128, 128], BF16))
    convw = A(nc.sbuf_tensor("convw_s", [128, EC * 4], F32))
    convb = A(nc.sbuf_tensor("convb_s", [128, EC], F32))
    dvec = A(nc.sbuf_tensor("dvec_s", [128, EC], F32))
    bdt = A(nc.sbuf_tensor("bdt_s", [16, 1], F32))
    acol = A(nc.sbuf_tensor("acol_s", [16, 1], F32))
    zer16 = A(nc.sbuf_tensor("zer16_s", [16, 128], F32))

    PB = [A(nc.psum_tensor(f"pb{j}", [128, 512], F32)) for j in range(8)]
    bank_tok = [None] * 8
    rot_mm = [0]
    rot_tr = [0]
    rot_y = [0]
    TRB = (2, 3, 4, 5)

    def take(rot, banks):
        b = banks[rot[0] % len(banks)]
        rot[0] += 1
        return b

    ctxA = contextlib.ExitStack()
    AA = ctxA.enter_context
    win = AA(nc.sbuf_tensor("win_s", [128, DC * 4096], BF16))
    wout = AA(nc.sbuf_tensor("wout_s", [128, EC * 1024], BF16))
    wxd = AA(nc.sbuf_tensor("wxd_s", [128, EC * 48], BF16))
    xt = AA(nc.sbuf_tensor("xt_s", [128, DC * T], F32))
    xs2 = AA(nc.sbuf_tensor("xs2_s", [128, DC * 512], BF16))
    xn = [AA(nc.sbuf_tensor(f"xn{p}", [128, DC * T], BF16)) for p in range(2)]
    u = [AA(nc.sbuf_tensor(f"u{p}", [128, EC * T], BF16)) for p in range(2)]
    tri_unused = None
    ut = AA(nc.sbuf_tensor("ut_s", [128, 2 * E], BF16))
    sz = AA(nc.sbuf_tensor("sz_s", [128, EC * T], BF16))
    y1 = AA(nc.sbuf_tensor("y1_s", [128, EC * T], BF16))
    xib = AA(nc.sbuf_tensor("xib_s", [128, 4 * (T + 3)], BF16))
    xtl = AA(nc.sbuf_tensor("xtl_s", [128, EC * 3], BF16))
    cacc = AA(nc.sbuf_tensor("cacc_s", [128, 4 * T], BF16))
    attn_st = AA(nc.sbuf_tensor("attn_st", [128, DC * T], BF16))
    rep = AA(nc.sbuf_tensor("rep_s", [128, 4 * 128], F32))
    smx = AA(nc.sbuf_tensor("smx_s", [1, 768], F32))

    mct = AA(nc.sbuf_tensor("mct_s", [128, 4], F32))
    NSM = 12
    sma = [AA(nc.sbuf_tensor(f"sma{j}", [16, 128], F32)) for j in range(NSM)]
    bcs = AA(nc.sbuf_tensor("bcs_s", [16, 512], F32))
    _p2a = [AA(nc.sbuf_tensor(f"p2a{q}", [16, 256], F32)) for q in range(2)]
    pack2a = [_p2a, _p2a]
    pack2b = [AA(nc.sbuf_tensor(f"p2b{q}", [16, 256], F32)) for q in range(2)]

    packt4 = [[AA(nc.sbuf_tensor(f"packt{p}{q}", [128, 64], F32))
               for q in range(2)] for p in range(2)]
    dbtn4 = [[AA(nc.sbuf_tensor(f"dbtn{p}{q}", [128, 16], F32))
              for q in range(2)] for p in range(2)]
    epl = [AA(nc.sbuf_tensor(f"epl{p}", [128, 512], BF16)) for p in range(2)]
    macc = AA(nc.sbuf_tensor("macc", [128, 128], F32))
    macc2 = AA(nc.sbuf_tensor("macc2", [128, 128], F32))
    mmb2 = [AA(nc.sbuf_tensor(f"mmb{q}", [128, 128], BF16)) for q in range(2)]
    hst = AA(nc.sbuf_tensor("hst", [16, E], F32))
    hbf = AA(nc.sbuf_tensor("hbf", [16, E], BF16))
    ccum4 = [[AA(nc.sbuf_tensor(f"ccum{p}{q}", [16, 128], BF16))
              for q in range(2)] for p in range(2)]
    ebs2 = [AA(nc.sbuf_tensor(f"ebs{p}", [16, 128], BF16)) for p in range(2)]
    bsc4 = [[AA(nc.sbuf_tensor(f"bsc{p}{q}", [128, 16], BF16))
             for q in range(2)] for p in range(2)]
    dec4 = [[AA(nc.sbuf_tensor(f"dec{p}{q}", [16, 1], F32))
             for q in range(2)] for p in range(2)]

    t_consts = S.dma("sy", lambda e: e.dma_start(out=identf[:], in_=d_identf))
    t_cb = S.dma("sy", lambda e: e.dma_start(out=identb[:], in_=d_identb))
    t_tri = None
    t_ones = S.dma("sy", lambda e: e.dma_start(out=onesb[:], in_=d_onesb))
    t_cw = S.dma("sy", lambda e: e.dma_start(out=convw[:], in_=d_convw))
    t_cbi = S.dma("sy", lambda e: e.dma_start(out=convb[:], in_=d_convb))
    t_dv = S.dma("sy", lambda e: e.dma_start(out=dvec[:], in_=d_dvec))
    t_bdt = S.dma("sy", lambda e: e.dma_start(out=bdt[:], in_=d_bdt))
    t_alog = S.dma("sy", lambda e: e.dma_start(out=acol[:], in_=d_alog))
    t_win = [S.dma("gp", lambda e, c=c: e.dma_start(
        out=win[:, c * 8192:(c + 1) * 8192], in_=d_win[:, c * 8192:(c + 1) * 8192]))
        for c in range(4)]
    t_wout = [S.dma("gp", lambda e, c=c: e.dma_start(
        out=wout[:, c * 8192:(c + 1) * 8192], in_=d_wout[:, c * 8192:(c + 1) * 8192]))
        for c in range(2)]
    t_wxd = S.dma("gp", lambda e: e.dma_start(out=wxd[:], in_=d_wxd))
    t_ac1 = S.op("scalar", lambda e: e.activation(out=acol[:], in_=acol[:], func=AF.Exp),
                 deps=[t_alog])
    t_ac2 = S.op("vector", lambda e: e.tensor_scalar(
        out=acol[:], in0=acol[:], scalar1=-1.0, scalar2=None, op0=ALU.mult),
        deps=[t_ac1])
    t_z16 = S.op("gpsimd", lambda e: e.memset(zer16[:], 0.0))
    t_h0 = S.op("gpsimd", lambda e: e.memset(hst[:], 0.0))
    t_hb0 = S.op("gpsimd", lambda e: e.memset(hbf[:], 0.0))
    t_xtl0 = S.op("gpsimd", lambda e: e.memset(xtl[:], 0.0))
    t_zz0 = S.dma("sy", lambda e: e.dma_start(out=mneg[:], in_=d_mneg))

    tok = {
        "xt_free": None, "xn_free": [None, None], "u_free": [None, None],
        "sz_free": None, "y1_free": None, "attn_free": None,
        "bcs_dt": None, "bcs_b": None, "bcs_c": None,
        "hb": t_hb0, "h": t_h0, "xtl": [t_xtl0] * EC,
        "xib": [None] * 4, "cacc": [None] * 4, "sm": None, "rep": None,
        "pack": [[None] * 2 for _ in range(2)],
        "packb": [None, None],
        "packt_evac": [[None] * 2 for _ in range(2)],
        "epl": [None, None], "macc": None, "mmb": [None, None],
        "small": [None] * NSM, "ccum": [[None] * 2 for _ in range(2)],
        "ebs": [None, None],
        "bsc": [[None] * 2 for _ in range(2)],
        "dec": [[None] * 2 for _ in range(2)], "xs2": None, "mct": None,
        "mc2": None, "dbtn": [[None] * 2 for _ in range(2)],
        "ut_free2": [None, None],
    }

    sb1, sb2 = all_barrier(S, 0, {
        "sync": [t_consts, t_cb, t_ones, t_cw, t_cbi, t_dv, t_bdt,
                 t_alog, t_zz0],
        "gpsimd": t_win + t_wout + [t_wxd],
        "vector": [t_ac2],
    })
    wAll = list(sb2.values())
    del t_tri

    def vclamp(*xs):
        out = []
        for x in xs:
            if x is None:
                continue
            if isinstance(x, list):
                out += [y for y in x if y is not None]
            else:
                out.append(x)
        return out

    fe_state = {}
    spp_state = {}
    spd = {}
    spd2 = {}

    def emit_FE_pre(it):
        par = it % 2
        tile0 = it * T

        xin = d_x.rearrange("p (c t) -> p c t", c=DC)[:, :, tile0:tile0 + T]
        t_xt = S.dma("sy", lambda e, xin=xin: e.dma_start(
            out=xt[:].rearrange("p (c t) -> p c t", c=DC), in_=xin),
            deps=vclamp(tok["xt_free"]))

        xs2v = xs2[:].rearrange("p (c t) -> p c t", c=DC)
        w0 = wAll if it == 0 else []
        t_cp = S.op("scalar", lambda e: e.copy(
            xs2v[:, :, 0:T], xt[:].rearrange("p (c t) -> p c t", c=DC)),
            deps=[t_xt] + vclamp(tok["xs2"]) + w0)
        t_sq = S.op("scalar", lambda e: e.activation(
            out=xs2v[:, :, T:512], in_=xt[:].rearrange("p (c t) -> p c t", c=DC),
            func=AF.Square), deps=[t_xt] + vclamp(tok["xs2"]) + w0)

        fe_state[it] = dict(t_cp=t_cp, t_sq=t_sq, w0=w0)

    def emit_FE_stats(it):
        par = it % 2
        tile0 = it * T
        t_cp = fe_state[it]["t_cp"]
        t_sq = fe_state[it]["t_sq"]
        w0 = fe_state[it]["w0"]

        bs = take(rot_mm, (0, 1))
        dep0 = vclamp(bank_tok[bs]) + [t_cp, t_sq] + w0
        t_stat = None
        for k in range(DC):
            t_stat = S.op("tensor", lambda e, k=k, bs=bs: e.matmul(
                PB[bs][0:1, 0:512], onesb[:], xs2[:, k * 512:(k + 1) * 512],
                start=(k == 0), stop=(k == DC - 1)),
                deps=(dep0 if k == 0 else []))
        tok["xs2"] = t_stat

        t_m = S.op("vector", lambda e, bs=bs: e.tensor_scalar(
            out=smx[0:1, 0:256], in0=PB[bs][0:1, 0:T], scalar1=1.0 / D,
            scalar2=None,
            op0=ALU.mult), deps=[t_stat] + vclamp(tok["sm"]))
        t_e2 = S.op("vector", lambda e, bs=bs: e.tensor_scalar(
            out=smx[0:1, 256:512], in0=PB[bs][0:1, T:512], scalar1=1.0 / D,
            scalar2=1e-5, op0=ALU.mult, op1=ALU.add), deps=[t_stat])
        bank_tok[bs] = t_e2
        t_msq = S.op("vector", lambda e: e.tensor_mul(
            smx[0:1, 512:768], smx[0:1, 0:256], smx[0:1, 0:256]),
            deps=[t_m])
        t_vr = S.op("vector", lambda e: e.tensor_sub(
            smx[0:1, 256:512], smx[0:1, 256:512], smx[0:1, 512:768]),
            deps=[t_e2, t_msq])
        t_ln = S.op("scalar", lambda e: e.activation(
            out=smx[0:1, 256:512], in_=smx[0:1, 256:512], func=AF.Ln),
            deps=[t_vr])
        t_rstd = S.op("scalar", lambda e: e.activation(
            out=smx[0:1, 512:768], in_=smx[0:1, 256:512], func=AF.Exp,
            scale=-0.5),
            deps=[t_ln])
        t_c = S.op("vector", lambda e: e.scalar_tensor_tensor(
            out=smx[0:1, 0:256], in0=smx[0:1, 0:256], scalar=-1.0,
            in1=smx[0:1, 512:768], op0=ALU.mult, op1=ALU.mult), deps=[t_rstd])

        bt = take(rot_tr, TRB)
        t_mcT = []
        pieces = [(512, 0), (0, 1), (640, 2), (128, 3)]
        for sc, dc_ in pieces:
            t_mcT.append(S.op("tensor", lambda e, sc=sc, dc_=dc_, bt=bt:
                              e.transpose(
                PB[bt][:, dc_:dc_ + 1], smx[0:1, sc:sc + 128],
                identf[0:1, 0:1]),
                deps=vclamp(bank_tok[bt]) + [t_rstd, t_c]))
        tok["sm"] = t_mcT
        t_mcE = S.op("scalar", lambda e, bt=bt: e.copy(mct[:], PB[bt][:, 0:4]),
                     deps=t_mcT + (tok["mct"] or []))
        bank_tok[bt] = t_mcE

        t_repE = []
        rep_tts = []
        for gstart in (0, 2):
            bt = take(rot_tr, TRB)
            tts = []
            for j in (gstart, gstart + 1):
                tts.append(S.op("tensor", lambda e, j=j, g=gstart, bt=bt: e.transpose(
                    PB[bt][:, (j - g) * 128:(j - g + 1) * 128],
                    mct[:, j:j + 1].broadcast_to((128, 128)), identf[:]),
                    deps=vclamp(bank_tok[bt]) + [t_mcE]))
            rep_tts += tts
            t_repE.append(S.op("scalar", lambda e, g=gstart, bt=bt: e.copy(
                rep[:, g * 128:(g + 2) * 128], PB[bt][:, 0:256]),
                deps=tts + vclamp(tok["rep"])))
            bank_tok[bt] = t_repE[-1]
        tok["mct"] = rep_tts
        fe_state[it].update(t_repE=t_repE)

    def emit_FE_apply(it):
        par = it % 2
        tile0 = it * T
        t_repE = fe_state[it]["t_repE"]
        xtv = xt[:].rearrange("p (c t) -> p c t", c=DC)
        xnv = xn[par][:].rearrange("p (c t) -> p c t", c=DC)
        repv = rep[:].rearrange("p (g t) -> p g t", g=4)
        t_apply = []
        for h in range(2):
            rsl = repv[:, 2 * h:2 * h + 1, :].broadcast_to((128, DC, 128))
            csl = repv[:, 2 * h + 1:2 * h + 2, :].broadcast_to((128, DC, 128))
            t_a1 = S.op("vector", lambda e, h=h, rsl=rsl, xnv=xnv: e.tensor_mul(
                xnv[:, :, h * 128:(h + 1) * 128],
                xtv[:, :, h * 128:(h + 1) * 128], rsl),
                deps=[t_repE[h]] + vclamp(tok["xn_free"][par]))
            t_a2 = S.op("vector", lambda e, h=h, csl=csl, xnv=xnv: e.tensor_add(
                xnv[:, :, h * 128:(h + 1) * 128],
                xnv[:, :, h * 128:(h + 1) * 128], csl),
                deps=[t_a1])
            t_apply.append(t_a2)
        tok["rep"] = t_apply[-1]
        tok["xt_free"] = t_apply[-1]

        xnd = d_xn.rearrange("p (c t) -> p c t", c=DC)[:, :, tile0:tile0 + T]
        t_xnw = S.dma("gp", lambda e, xnd=xnd, par=par: e.dma_start(
            out=xnd, in_=xn[par][:].rearrange("p (c t) -> p c t", c=DC)),
            deps=t_apply)

        fe_state[it].update(t_apply=t_apply, t_xnw=t_xnw)

    def emit_FE_xz(it):
        par = it % 2
        t_apply = fe_state[it]["t_apply"]
        t_u = [None] * EC
        for m in range(EC):
            bs = take(rot_mm, (0, 1))
            t_mm = None
            mmdep = vclamp(bank_tok[bs]) + t_apply
            for k in range(DC):
                t_mm = S.op("tensor", lambda e, m=m, k=k, bs=bs, par=par: e.matmul(
                    PB[bs][:, 0:T],
                    win[:, k * 4096 + m * 128:k * 4096 + (m + 1) * 128],
                    xn[par][:, k * T:(k + 1) * T],
                    start=(k == 0), stop=(k == DC - 1)),
                    deps=(mmdep if k == 0 else []))
            c = m
            sl = (c % 4) * (T + 3)
            sa = (c % 4) * T
            t_tr = S.op("scalar", lambda e, c=c, sl=sl: e.copy(
                xib[:, sl:sl + 3], xtl[:, c * 3:(c + 1) * 3]),
                deps=vclamp(tok["xib"][c % 4], tok["xtl"][c]))
            t_ev = S.op("scalar", lambda e, bs=bs, sl=sl: e.copy(
                xib[:, sl + 3:sl + T + 3], PB[bs][:, 0:T]),
                deps=[t_mm] + vclamp(tok["xib"][c % 4]))
            bank_tok[bs] = t_ev
            t_ts = S.op("scalar", lambda e, c=c, sl=sl: e.copy(
                xtl[:, c * 3:(c + 1) * 3], xib[:, sl + T:sl + T + 3]),
                deps=[t_ev, t_tr])
            tok["xtl"][c] = t_ts
            t_c0 = S.op("vector", lambda e, c=c, sl=sl, sa=sa: e.tensor_scalar(
                out=cacc[:, sa:sa + T], in0=xib[:, sl:sl + T],
                scalar1=convw[:, 4 * c:4 * c + 1],
                scalar2=convb[:, c:c + 1], op0=ALU.mult, op1=ALU.add),
                deps=[t_ev, t_tr] + vclamp(tok["cacc"][c % 4]))
            t_ck = t_c0
            for k in range(1, 4):
                t_ck = S.op("vector", lambda e, c=c, k=k, sl=sl, sa=sa:
                            e.scalar_tensor_tensor(
                    out=cacc[:, sa:sa + T], in0=xib[:, sl + k:sl + k + T],
                    scalar=convw[:, 4 * c + k:4 * c + k + 1],
                    in1=cacc[:, sa:sa + T],
                    op0=ALU.mult, op1=ALU.add), deps=[t_ck])
            tok["xib"][c % 4] = t_ck
            if c % 2 == 1:
                t_up = S.op("scalar", lambda e, c=c, sa=sa, par=par: e.activation(
                    out=u[par][:, (c - 1) * T:(c + 1) * T],
                    in_=cacc[:, sa - T:sa + T], func=AF.Silu),
                    deps=[t_ck, prev_ck] + vclamp(tok["u_free"][par]))
                t_u[c - 1] = t_up
                t_u[c] = t_up
                tok["cacc"][(c - 1) % 4] = t_up
            prev_ck = t_ck
            if c % 2 == 1:
                tok["cacc"][c % 4] = t_u[c]
        fe_state[it].update(t_u=t_u)

    def emit_SPP(it):
        par = it % 2
        tp = it % 2
        t_u = fe_state[it]["t_u"]

        bwB = take(rot_tr, TRB)
        t_g = None
        for k in range(EC):
            kdep = [t_u[k]] + (vclamp(bank_tok[bwB]) if k == 0 else [])
            t_g = S.op("tensor", lambda e, k=k, bwB=bwB, par=par: e.matmul(
                PB[bwB][0:16, 0:T],
                wxd[:, k * 48 + 32:k * 48 + 48],
                u[par][:, k * T:(k + 1) * T],
                start=(k == 0), stop=(k == EC - 1)),
                deps=kdep)
        t_gdt = t_g
        bwA = take(rot_tr, TRB)
        tgs = []
        for j, col in ((0, 0), (1, 256)):
            t_g = None
            for k in range(EC):
                kdep = [t_u[k]] + (vclamp(bank_tok[bwA])
                                   if (j == 0 and k == 0) else [])
                t_g = S.op("tensor", lambda e, j=j, k=k, bwA=bwA, col=col,
                           par=par: e.matmul(
                    PB[bwA][0:16, col:col + T],
                    wxd[:, k * 48 + j * 16:k * 48 + (j + 1) * 16],
                    u[par][:, k * T:(k + 1) * T],
                    start=(k == 0), stop=(k == EC - 1)),
                    deps=kdep)
            tgs.append(t_g)

        bc_readers = []
        dt_readers = []
        front = [None, None]
        for q in range(2):
            qs = q * Q
            o = 6 * q
            sm = sma[o:o + 6]
            t_d1 = S.op("scalar", lambda e, qs=qs, sm=sm, bwB=bwB:
                        e.activation(
                out=sm[0][:], in_=PB[bwB][0:16, qs:qs + Q], func=AF.Abs,
                bias=bdt[:]), deps=[t_gdt] + vclamp(tok["small"][o + 0]))
            t_d2 = S.op("scalar", lambda e, sm=sm: e.activation(
                out=sm[0][:], in_=sm[0][:], func=AF.Exp, scale=-1.0), deps=[t_d1])
            t_d3 = S.op("scalar", lambda e, sm=sm: e.activation(
                out=sm[0][:], in_=sm[0][:], func=AF.Ln, bias=1.0), deps=[t_d2])
            t_d4 = S.op("scalar", lambda e, qs=qs, sm=sm, bwB=bwB:
                        e.activation(
                out=sm[1][:], in_=PB[bwB][0:16, qs:qs + Q], func=AF.Relu,
                bias=bdt[:]), deps=[t_gdt] + vclamp(tok["small"][o + 1]))
            t_delta = S.op("vector", lambda e, sm=sm: e.tensor_add(
                sm[2][:], sm[0][:], sm[1][:]),
                deps=[t_d3, t_d4] + vclamp(tok["small"][o + 2]))
            tok["small"][o + 0] = t_delta
            tok["small"][o + 1] = t_delta
            t_dal = S.op("vector", lambda e, sm=sm: e.tensor_scalar(
                out=sm[3][:], in0=sm[2][:], scalar1=acol[:], scalar2=None,
                op0=ALU.mult), deps=[t_delta] + vclamp(tok["small"][o + 3]))
            t_S = S.op("vector", lambda e, sm=sm: e.tensor_tensor_scan(
                out=sm[4][:], data0=sm[3][:], data1=zer16[:], initial=0.0,
                op0=ALU.add, op1=ALU.add),
                deps=[t_dal] + vclamp(tok["small"][o + 4]))
            tok["small"][o + 3] = t_S
            front[q] = dict(t_delta=t_delta, t_S=t_S, t_d1=t_d1, t_d4=t_d4)
        bank_tok[bwB] = [front[0]["t_d1"], front[0]["t_d4"],
                         front[1]["t_d1"], front[1]["t_d4"]]
        for q in range(2):
            o = 6 * q
            sm = sma[o:o + 6]
            fr = front[q]
            t_S = fr["t_S"]
            t_p2 = S.op("vector", lambda e, sm=sm, pb=pack2b[q]: e.tensor_scalar(
                out=pb[:, 0:128], in0=sm[4][:], scalar1=-1.0, scalar2=None,
                op0=ALU.mult), deps=[t_S] + vclamp(tok["packb"][q]))
            t_eS = S.op("scalar", lambda e, sm=sm: e.activation(
                out=sm[1][:], in_=sm[4][:], func=AF.Exp),
                deps=[t_S] + vclamp(tok["small"][o + 1]))
            t_a2s = S.op("vector", lambda e, sm=sm: e.tensor_scalar(
                out=sm[3][:], in0=sm[4][:], scalar1=-1.0,
                scalar2=sm[4][:, 127:128], op0=ALU.mult, op1=ALU.add),
                deps=[t_S] + vclamp(tok["small"][o + 3]))
            t_e2s = S.op("scalar", lambda e, sm=sm: e.activation(
                out=sm[3][:], in_=sm[3][:], func=AF.Exp), deps=[t_a2s])
            t_dec = S.op("scalar", lambda e, sm=sm, dec=dec4[tp][q]:
                         e.activation(
                out=dec[:], in_=sm[4][:, 127:128], func=AF.Exp),
                deps=[t_S] + vclamp(tok["dec"][tp][q]))
            fr.update(t_p2=t_p2, t_eS=t_eS, t_e2s=t_e2s, t_dec=t_dec)

        t_bc0 = S.op("scalar", lambda e, bwA=bwA: e.copy(
            bcs[0:16, 0:T], PB[bwA][0:16, 0:T]),
            deps=[tgs[0], tgs[1]] + vclamp(tok["bcs_b"]))
        t_bc1 = S.op("scalar", lambda e, bwA=bwA: e.copy(
            bcs[0:16, T:2 * T], PB[bwA][0:16, T:2 * T]),
            deps=[tgs[1], tgs[0]] + vclamp(tok["bcs_c"]))
        bank_tok[bwA] = t_bc1

        back = [None, None]
        for q in range(2):
            qs = q * Q
            o = 6 * q
            sm = sma[o:o + 6]
            fr = front[q]
            t_S = fr["t_S"]
            t_dB = S.op("vector", lambda e, qs=qs, sm=sm, pb=pack2b[q]:
                        e.tensor_mul(
                pb[:, 128:256], sm[2][:], bcs[0:16, qs:qs + Q]),
                deps=[t_bc0, fr["t_delta"]] + vclamp(tok["packb"][q]))
            tok["small"][o + 2] = t_dB
            bc_readers.append(t_dB)
            t_cp = S.op("vector", lambda e, qs=qs, sm=sm: e.tensor_scalar(
                out=sm[5][:], in0=bcs[0:16, 256 + qs:256 + qs + Q],
                scalar1=1e-38, scalar2=None, op0=ALU.max),
                deps=[t_bc1] + vclamp(tok["small"][o + 5]))
            t_lcp = S.op("scalar", lambda e, sm=sm: e.activation(
                out=sm[5][:], in_=sm[5][:], func=AF.Ln), deps=[t_cp])
            t_p0 = S.op("vector", lambda e, sm=sm, pa=pack2a[tp][q]:
                        e.tensor_add(
                pa[:, 0:128], sm[4][:], sm[5][:]),
                deps=[t_lcp, t_S] + vclamp(tok["pack"][tp][q]))
            t_cm = S.op("vector", lambda e, qs=qs, sm=sm: e.tensor_scalar(
                out=sm[0][:], in0=bcs[0:16, 256 + qs:256 + qs + Q],
                scalar1=-1.0, scalar2=1e-38, op0=ALU.mult, op1=ALU.max),
                deps=[t_bc1, fr["t_delta"]] + vclamp(tok["small"][o + 0]))
            t_lcm = S.op("scalar", lambda e, sm=sm: e.activation(
                out=sm[0][:], in_=sm[0][:], func=AF.Ln), deps=[t_cm])
            t_p1 = S.op("vector", lambda e, sm=sm, pa=pack2a[tp][q]:
                        e.tensor_add(
                pa[:, 128:256], sm[4][:], sm[0][:]), deps=[t_lcm])
            bc_readers += [t_cp, t_cm]
            t_ccum = S.op("vector", lambda e, qs=qs, sm=sm, ccum=ccum4[tp][q]:
                          e.tensor_mul(
                ccum[:], bcs[0:16, 256 + qs:256 + qs + Q], sm[1][:]),
                deps=[fr["t_eS"], t_bc1] + vclamp(tok["ccum"][tp][q]))
            bc_readers.append(t_ccum)
            tok["small"][o + 1] = t_ccum
            t_ebs = S.op("vector", lambda e, sm=sm, pb=pack2b[q],
                         eb=ebs2[q]: e.tensor_mul(
                eb[:], sm[3][:], pb[:, 128:256]),
                deps=[fr["t_e2s"], t_dB] + vclamp(tok["ebs"][q]))
            tok["small"][o + 3] = t_ebs
            tok["small"][o + 0] = t_p1
            tok["small"][o + 4] = [t_p0, t_p1, fr["t_dec"]]
            tok["small"][o + 5] = t_p0
            back[q] = dict(t_dB=t_dB, t_p0=t_p0, t_p1=t_p1, t_ccum=t_ccum,
                           t_ebs=t_ebs)

        tok["bcs_b"] = [back[0]["t_dB"], back[1]["t_dB"]]
        tok["bcs_c"] = bc_readers
        spp_state[it] = dict(front=front, back=back, t_u=t_u)

    def emit_SPP_T(it):
        par = it % 2
        tp = it % 2
        st = spp_state.pop(it)
        front = st["front"]
        back = st["back"]
        t_u = st["t_u"]
        for q in range(2):
            packt = packt4[tp][q]
            dbtn = dbtn4[tp][q]
            fr = front[q]
            bk = back[q]
            t_ebs = bk["t_ebs"]
            bt = take(rot_tr, TRB)
            t_pT = []
            for g in range(4):
                srcap = (pack2a[tp][q][:, (g % 2) * 128:(g % 2) * 128 + 128]
                         if g < 2 else
                         pack2b[q][:, (g - 2) * 128:(g - 2) * 128 + 128])
                t_pT.append(S.op("tensor", lambda e, g=g, bt=bt, srcap=srcap:
                            e.transpose(
                    PB[bt][:, g * 16:(g + 1) * 16], srcap,
                    identf[0:16, 0:16]),
                    deps=vclamp(bank_tok[bt]) +
                         [bk["t_dB"], bk["t_p0"], bk["t_p1"], fr["t_p2"]]))
            t_pE = S.op("scalar", lambda e, bt=bt, packt=packt: e.copy(
                packt[:], PB[bt][:, 0:64]),
                deps=t_pT + vclamp(tok["packt_evac"][tp][q]))
            bank_tok[bt] = t_pE
            t_dbtn = S.op("vector", lambda e, packt=packt, dbtn=dbtn:
                          e.tensor_scalar(
                out=dbtn[:], in0=packt[:, 48:64], scalar1=-1.0, scalar2=None,
                op0=ALU.mult), deps=[t_pE] + vclamp(tok["dbtn"][tp][q]))
            bt = take(rot_tr, TRB)
            t_bT = S.op("tensor", lambda e, bt=bt, eb=ebs2[q]: e.transpose(
                PB[bt][:].bitcast(BF16)[:, 0:16], eb[:], identb[0:16, 0:16]),
                deps=vclamp(bank_tok[bt]) + [t_ebs])
            t_bE = S.op("scalar", lambda e, bt=bt, bsc=bsc4[tp][q]: e.copy(
                bsc[:], PB[bt][:].bitcast(BF16)[:, 0:16]),
                deps=[t_bT] + vclamp(tok["bsc"][tp][q]))
            bank_tok[bt] = t_bE
            tok["ebs"][q] = t_bT
            tok["packb"][q] = [t_pT[2], t_pT[3], t_ebs]
            tok["pack"][tp][q] = [t_pT[0], t_pT[1]]
            spd[(it, q)] = dict(t_pE=t_pE, t_dbtn=t_dbtn, t_ccum=bk["t_ccum"],
                                t_bE=t_bE, t_dec=fr["t_dec"],
                                t_rows=[bk["t_p0"], bk["t_p1"]], t_u=t_u)


    def emit_PLANES1(it):
        par = it % 2
        tp = it % 2
        y_evacs = []
        for q in range(2):
            qs = q * Q
            packt = packt4[tp][q]
            dbtn = dbtn4[tp][q]
            ccum = ccum4[tp][q]
            bsc = bsc4[tp][q]
            dec = dec4[tp][q]
            st = spd.pop((it, q))
            t_pE = st["t_pE"]
            t_dbtn = st["t_dbtn"]
            t_ccum = st["t_ccum"]
            t_bE = st["t_bE"]
            t_dec = st["t_dec"]
            t_rows = st["t_rows"]
            t_u = st["t_u"]
            spd2[(it, q)] = st
            ut_evacs = []
            for g4 in range(4):
                bt = take(rot_tr, TRB)
                tts = []
                for j in range(4):
                    c = g4 * 4 + j
                    tts.append(S.op("tensor", lambda e, c=c, j=j, bt=bt, qs=qs,
                                    par=par: e.transpose(
                        PB[bt][:].bitcast(BF16)[:, j * 128:(j + 1) * 128],
                        u[par][:, c * T + qs:c * T + qs + Q], identb[:]),
                        deps=vclamp(bank_tok[bt]) + [t_u[c]]))
                if g4 % 2 == 0:
                    ut_evacs.append(S.op("scalar", lambda e, g4=g4, q=q, bt=bt:
                                         e.copy(
                        ut[:, q * E + g4 * 512:q * E + (g4 + 1) * 512],
                        PB[bt][:].bitcast(BF16)[:, 0:512]),
                        deps=tts + vclamp(tok["ut_free2"][q])))
                else:
                    ut_evacs.append(S.op("vector", lambda e, g4=g4, q=q, bt=bt:
                                         e.tensor_copy(
                        ut[:, q * E + g4 * 512:q * E + (g4 + 1) * 512],
                        PB[bt][:].bitcast(BF16)[:, 0:512]),
                        deps=tts + vclamp(tok["ut_free2"][q])))
                bank_tok[bt] = ut_evacs[-1]
            spd2[(it, q)]["ut_evacs"] = ut_evacs
            t_comb = None
            for grp in range(8):
                gbase = grp * 4
                bt = take(rot_tr, TRB)
                rrp = grp % 2
                tts = []
                for j in range(4):
                    g = gbase + j
                    col = (g // 16) * 16 + (g % 16)
                    tts.append(S.op("tensor", lambda e, col=col, j=j, bt=bt,
                                    packt=packt: e.matmul(
                        PB[bt][:, j * 128:(j + 1) * 128],
                        packt[:, col:col + 1].broadcast_to((128, 128)),
                        identf[:], is_transpose=True,
                        start=(j == 0), stop=False),
                        deps=vclamp(bank_tok[bt]) + [t_pE]))
                tts.append(S.op("tensor", lambda e, bt=bt: e.matmul(
                    PB[bt][:, 0:512], identb[:],
                    mneg[:].unsqueeze(1).broadcast_to((128, 4, 128)),
                    start=False, stop=True),
                    deps=[]))
                t_exs = []
                for j in range(4):
                    g = gbase + j
                    n = g % 16
                    t_exs.append(S.op("scalar", lambda e, j=j, n=n, rrp=rrp,
                                      bt=bt, packt=packt: e.activation(
                        out=epl[rrp][:, j * 128:(j + 1) * 128],
                        in_=PB[bt][:, j * 128:(j + 1) * 128], func=AF.Exp,
                        bias=packt[:, 32 + n:33 + n]),
                        deps=[tts[-1], t_pE] + vclamp(tok["epl"][rrp])))
                bank_tok[bt] = t_exs[-1]
                ma = macc if gbase < 16 else macc2
                for j in range(4):
                    g = gbase + j
                    n = g % 16
                    sc = (dbtn if g >= 16 else packt)
                    scol = (n if g >= 16 else 48 + n)
                    first = (g == 0 or g == 16)
                    scdep = t_dbtn if g >= 16 else t_pE
                    if first:
                        t_comb = S.op("vector", lambda e, j=j, sc=sc, scol=scol,
                                      rrp=rrp, ma=ma: e.tensor_scalar(
                            out=ma[:], in0=epl[rrp][:, j * 128:(j + 1) * 128],
                            scalar1=sc[:, scol:scol + 1], scalar2=None,
                            op0=ALU.mult),
                            deps=[t_exs[j], scdep] + vclamp(tok["macc"]))
                    else:
                        t_comb = S.op("vector", lambda e, j=j, sc=sc, scol=scol,
                                      rrp=rrp, ma=ma: e.scalar_tensor_tensor(
                            out=ma[:], in0=epl[rrp][:, j * 128:(j + 1) * 128],
                            scalar=sc[:, scol:scol + 1], in1=ma[:],
                            op0=ALU.mult, op1=ALU.add),
                            deps=[t_exs[j], t_comb, scdep])
                    if g == 15:
                        t_comb_a = t_comb
                tok["epl"][rrp] = t_comb
            tok["dbtn"][tp][q] = t_comb
            tok["packt_evac"][tp][q] = t_comb
            t_mm2 = S.op("vector", lambda e, mmb=mmb2[q]: e.tensor_add(
                mmb[:], macc[:], macc2[:]),
                deps=[t_comb_a, t_comb] + vclamp(tok["mmb"][q]))
            tok["macc"] = t_mm2
            spd2[(it, q)]["t_mm2"] = t_mm2
    def emit_PLANES2(it):
        par = it % 2
        tp = it % 2
        y_evacs = []
        for q in range(2):
            qs = q * Q
            ccum = ccum4[tp][q]
            bsc = bsc4[tp][q]
            dec = dec4[tp][q]
            st = spd2.pop((it, q))
            t_ccum = st["t_ccum"]
            t_bE = st["t_bE"]
            t_dec = st["t_dec"]
            t_u = st["t_u"]
            t_mm2 = st["t_mm2"]
            ut_evacs = st["ut_evacs"]
            for g4 in range(4):
                by = take(rot_y, (6, 7))
                t_ymm = []
                for j in range(4):
                    c = g4 * 4 + j
                    t_i1 = S.op("tensor", lambda e, c=c, j=j, q=q, by=by: e.matmul(
                        PB[by][:, j * 128:(j + 1) * 128],
                        ut[:, q * E + c * 128:q * E + (c + 1) * 128], mmb2[q][:],
                        start=True, stop=False),
                        deps=vclamp(bank_tok[by]) + [t_mm2, ut_evacs[c // 4]])
                    t_i2 = S.op("tensor", lambda e, c=c, j=j, by=by, ccum=ccum:
                                e.matmul(
                        PB[by][:, j * 128:(j + 1) * 128],
                        hbf[0:16, c * 128:(c + 1) * 128], ccum[:],
                        start=False, stop=True),
                        deps=[t_ccum, tok["hb"]])
                    t_ymm.append(t_i2)
                t_ye = None
                for j in range(4):
                    c = g4 * 4 + j
                    t_ye = S.op("vector", lambda e, c=c, j=j, q=q, by=by, qs=qs,
                                par=par: e.scalar_tensor_tensor(
                        out=y1[:, c * T + qs:c * T + qs + Q],
                        in0=u[par][:, c * T + qs:c * T + qs + Q],
                        scalar=dvec[:, c:c + 1], in1=PB[by][:, j * 128:(j + 1) * 128],
                        op0=ALU.mult, op1=ALU.add),
                        deps=[t_ymm[-1], t_u[c]] + vclamp(tok["y1_free"]))
                    y_evacs.append(t_ye)
                bank_tok[by] = t_ye
            tok["mmb"][q] = y_evacs[-1]
            tok["ccum"][tp][q] = y_evacs[-1]
            t_hup = []
            for i in range(4):
                bs = take(rot_mm, (0, 1))
                t_hc = S.op("tensor", lambda e, i=i, q=q, bs=bs, bsc=bsc: e.matmul(
                    PB[bs][0:16, 0:512], bsc[:],
                    ut[:, q * E + i * 512:q * E + (i + 1) * 512],
                    start=True, stop=True),
                    deps=vclamp(bank_tok[bs]) + [t_bE, ut_evacs[i]])
                t_h = S.op("vector", lambda e, i=i, bs=bs, dec=dec:
                           e.scalar_tensor_tensor(
                    out=hst[0:16, i * 512:(i + 1) * 512],
                    in0=hst[0:16, i * 512:(i + 1) * 512], scalar=dec[:],
                    in1=PB[bs][0:16, 0:512], op0=ALU.mult, op1=ALU.add),
                    deps=[t_hc, t_dec, tok["h"]] +
                         ([tok["hb"]] if tok["hb"] is not None else []))
                bank_tok[bs] = t_h
                t_hup.append(t_h)
            tok["bsc"][tp][q] = t_hup[-1]
            tok["dec"][tp][q] = t_hup[-1]
            tok["h"] = t_hup[-1]
            t_hb = S.op("scalar", lambda e: e.copy(hbf[:], hst[:]),
                        deps=t_hup)
            tok["hb"] = t_hb
            tok["ut_free2"][q] = t_hb


        fe_state[it]["y_evacs"] = y_evacs
        tok["u_free"][par] = y_evacs[-1]

    def emit_TAIL_Z(it):
        par = it % 2
        tile0 = it * T
        t_apply = fe_state[it]["t_apply"]

        t_sz = [None] * EC
        for c in range(EC):
            m = EC + c
            bs = take(rot_mm, (0, 1))
            t_mm = None
            mmdep = vclamp(bank_tok[bs]) + t_apply
            for k in range(DC):
                t_mm = S.op("tensor", lambda e, m=m, k=k, bs=bs, par=par: e.matmul(
                    PB[bs][:, 0:T],
                    win[:, k * 4096 + m * 128:k * 4096 + (m + 1) * 128],
                    xn[par][:, k * T:(k + 1) * T],
                    start=(k == 0), stop=(k == DC - 1)),
                    deps=(mmdep if k == 0 else []))
            t_sz[c] = S.op("scalar", lambda e, c=c, bs=bs: e.activation(
                out=sz[:, c * T:(c + 1) * T], in_=PB[bs][:, 0:T], func=AF.Silu),
                deps=[t_mm] + vclamp(tok["sz_free"]))
            bank_tok[bs] = t_sz[c]

        fe_state[it]["t_sz"] = t_sz

    def emit_TAIL_R(it):
        par = it % 2
        tile0 = it * T
        t_sz = fe_state[it]["t_sz"]
        y_evacs = fe_state[it]["y_evacs"]
        t_xnw = fe_state[it]["t_xnw"]

        t_gk = [None] * EC
        for k in range(EC):
            t_gk[k] = S.op("vector", lambda e, k=k: e.tensor_mul(
                y1[:, k * T:(k + 1) * T], y1[:, k * T:(k + 1) * T],
                sz[:, k * T:(k + 1) * T]),
                deps=[y_evacs[k], y_evacs[16 + k], t_sz[k]])
        t_gate = t_gk[-1]
        tok["sz_free"] = list(t_gk)
        tok["u_free"][0] = t_gate
        tok["u_free"][1] = t_gate

        t_attn_dma_dep = []
        for m in range(DC):
            bs = take(rot_mm, (0, 1))
            t_mo = None
            modep = vclamp(bank_tok[bs])
            for k in range(EC):
                kdep = list(modep if k == 0 else [])
                if m == 0:
                    kdep.append(t_gk[k])
                t_mo = S.op("tensor", lambda e, m=m, k=k, bs=bs: e.matmul(
                    PB[bs][:, 0:T],
                    wout[:, k * 1024 + m * 128:k * 1024 + (m + 1) * 128],
                    y1[:, k * T:(k + 1) * T],
                    start=(k == 0), stop=(k == EC - 1)),
                    deps=kdep)
            t_ae = S.op("scalar", lambda e, m=m, bs=bs: e.copy(
                attn_st[:, m * T:(m + 1) * T], PB[bs][:, 0:T]),
                deps=[t_mo] + vclamp(tok["attn_free"]))
            bank_tok[bs] = t_ae
            t_attn_dma_dep.append(t_ae)
        attnd = d_attn.rearrange("p (c t) -> p c t", c=DC)[:, :, tile0:tile0 + T]
        t_attn_dma = S.dma("gp", lambda e, attnd=attnd: e.dma_start(
            out=attnd, in_=attn_st[:].rearrange("p (c t) -> p c t", c=DC)),
            deps=t_attn_dma_dep)
        tok["attn_free"] = t_attn_dma
        tok["xn_free"][par] = t_sz[-1]
        tok["y1_free"] = t_attn_dma_dep[-1]

    emit_FE_pre(0)
    emit_FE_stats(0)
    emit_FE_apply(0)
    emit_FE_xz(0)
    emit_SPP(0)
    emit_SPP_T(0)
    for it in range(NT):
        if it + 1 < NT:
            emit_FE_pre(it + 1)
            emit_FE_stats(it + 1)
            emit_FE_apply(it + 1)
        emit_PLANES1(it)
        if it + 1 < NT:
            emit_FE_xz(it + 1)
        emit_TAIL_Z(it)
        if it + 1 < NT:
            emit_SPP(it + 1)
        emit_PLANES2(it)
        emit_TAIL_R(it)
        if it + 1 < NT:
            emit_SPP_T(it + 1)

    barrier1, barrier2 = all_barrier(
        S, 2, {"sync": vclamp(tok["attn_free"], tok["xn_free"][0],
                              tok["xn_free"][1])})

    ctxA.close()
    AB = ctxA.enter_context
    wfc = AB(nc.sbuf_tensor("wfc_s", [128, DC * 4096], BF16))
    wproj = AB(nc.sbuf_tensor("wproj_s", [128, FC * 1024], BF16))
    xnb = AB(nc.sbuf_tensor("xnb_s", [128, DC * T], BF16))
    xb2 = AB(nc.sbuf_tensor("xb2_s", [128, DC * T], F32))
    atb = AB(nc.sbuf_tensor("atb_s", [128, DC * T], BF16))
    hsq = AB(nc.sbuf_tensor("hsq_s", [128, FC * T], BF16))
    hrel = AB(nc.sbuf_tensor("hrel_s", [128, T], BF16))
    yf = AB(nc.sbuf_tensor("yf_s", [128, DC * T], F32))
    mem = AB(nc.sbuf_tensor("mem_s", [128, DC * T], F32))
    accb2 = [AB(nc.sbuf_tensor(f"accb{p}", [128, DC * T], F32)) for p in range(2)]
    spk = AB(nc.sbuf_tensor("spk_s", [128, DC * T], F32))

    bdeps = list(barrier2.values())
    t_wfc = [S.dma("gp", lambda e, c=c: e.dma_start(
        out=wfc[:, c * 8192:(c + 1) * 8192], in_=d_wfc[:, c * 8192:(c + 1) * 8192]),
        deps=bdeps) for c in range(4)]
    t_wproj = [S.dma("gp", lambda e, c=c: e.dma_start(
        out=wproj[:, c * 8192:(c + 1) * 8192],
        in_=d_wproj[:, c * 8192:(c + 1) * 8192]), deps=bdeps) for c in range(4)]
    wB = t_wfc + t_wproj

    btok = {"xn": bdeps, "x": bdeps, "attn": bdeps, "hsq": None,
            "hrel": [None, None], "yf": [], "mem": [], "acc": [None, None],
            "spk": [], "out_dma": None}
    t_out_dmas = []
    for it in range(NT):
        tile0 = it * T
        xnd = d_xn.rearrange("p (c t) -> p c t", c=DC)[:, :, tile0:tile0 + T]
        t_xn = S.dma("sy", lambda e, xnd=xnd: e.dma_start(
            out=xnb[:].rearrange("p (c t) -> p c t", c=DC), in_=xnd),
            deps=btok["xn"] if isinstance(btok["xn"], list) else [btok["xn"]])
        xind = d_x.rearrange("p (c t) -> p c t", c=DC)[:, :, tile0:tile0 + T]
        t_x = S.dma("sy", lambda e, xind=xind: e.dma_start(
            out=xb2[:].rearrange("p (c t) -> p c t", c=DC), in_=xind),
            deps=btok["x"] if isinstance(btok["x"], list) else [btok["x"]])
        attnd = d_attn.rearrange("p (c t) -> p c t", c=DC)[:, :, tile0:tile0 + T]
        t_at = S.dma("sy", lambda e, attnd=attnd: e.dma_start(
            out=atb[:].rearrange("p (c t) -> p c t", c=DC), in_=attnd),
            deps=btok["attn"] if isinstance(btok["attn"], list) else [btok["attn"]])
        accb = accb2[it % 2]
        t_acc0 = S.op("vector", lambda e, accb=accb: e.tensor_add(
            accb[:], xb2[:], atb[:]),
            deps=[t_x, t_at] + vclamp(btok["acc"][it % 2]))
        btok["x"] = t_acc0
        btok["attn"] = t_acc0

        t_h = [None] * FC
        for m in range(FC):
            bs = take(rot_mm, (0, 1))
            t_mm = None
            mmdep = vclamp(bank_tok[bs]) + [t_xn]
            for k in range(DC):
                kdep = list(mmdep if k == 0 else [])
                if it == 0:
                    kdep.append(t_wfc[k // 2])
                t_mm = S.op("tensor", lambda e, m=m, k=k, bs=bs: e.matmul(
                    PB[bs][:, 0:T],
                    wfc[:, k * 4096 + m * 128:k * 4096 + (m + 1) * 128],
                    xnb[:, k * T:(k + 1) * T],
                    start=(k == 0), stop=(k == DC - 1)),
                    deps=kdep)
            sh = 0
            t_r = S.op("scalar", lambda e, bs=bs, sh=sh: e.activation(
                out=hrel[:, sh:sh + T], in_=PB[bs][:, 0:T], func=AF.Relu),
                deps=[t_mm] + vclamp(btok["hrel"][m % 2]))
            bank_tok[bs] = t_r
            t_h[m] = S.op("scalar", lambda e, m=m, sh=sh: e.activation(
                out=hsq[:, m * T:(m + 1) * T], in_=hrel[:, sh:sh + T],
                func=AF.Square), deps=[t_r] + vclamp(btok["hsq"]))
            btok["hrel"][m % 2] = t_h[m]
        btok["xn"] = t_h[-1]

        t_yf = []
        for m in range(DC):
            bs = take(rot_mm, (0, 1))
            t_mm = None
            for k in range(FC):
                kdep = [t_h[k]] + (vclamp(bank_tok[bs]) if k == 0 else [])
                if it == 0:
                    kdep.append(t_wproj[k // 8])
                t_mm = S.op("tensor", lambda e, m=m, k=k, bs=bs: e.matmul(
                    PB[bs][:, 0:T],
                    wproj[:, k * 1024 + m * 128:k * 1024 + (m + 1) * 128],
                    hsq[:, k * T:(k + 1) * T],
                    start=(k == 0), stop=(k == FC - 1)),
                    deps=kdep)
            t_yf.append(S.op("scalar", lambda e, m=m, bs=bs: e.copy(
                yf[:, m * T:(m + 1) * T], PB[bs][:, 0:T]),
                deps=[t_mm] + (btok["yf"] or [])))
            bank_tok[bs] = t_yf[-1]
        btok["hsq"] = t_yf[-1]

        yfdep = t_yf
        t_fin = []
        VS = DC * T
        for engn, c0, c1 in (("vector", 0, VS),):
            sl = slice(c0, c1)
            t_s = S.op(engn, lambda e, sl=sl: e.tensor_scalar(
                out=spk[:, sl], in0=yf[:, sl], scalar1=1.0, scalar2=0.2,
                op0=ALU.is_gt, op1=ALU.mult),
                deps=yfdep + (btok["spk"] or []))
            t_a = S.op(engn, lambda e, sl=sl, accb=accb: e.tensor_add(
                accb[:, sl], accb[:, sl], spk[:, sl]), deps=[t_s, t_acc0])
            t_m = S.op(engn, lambda e, sl=sl: e.tensor_scalar(
                out=mem[:, sl], in0=yf[:, sl], scalar1=1.9, scalar2=None,
                op0=ALU.mult), deps=yfdep + (btok["mem"] or []))
            for step in range(2, 6):
                if engn == "vector":
                    t_m = S.op(engn, lambda e, sl=sl: e.scalar_tensor_tensor(
                        out=mem[:, sl], in0=spk[:, sl], scalar=-5.0,
                        in1=mem[:, sl], op0=ALU.mult, op1=ALU.add),
                        deps=[t_a, t_m])
                else:
                    t_s = S.op(engn, lambda e, sl=sl: e.tensor_scalar(
                        out=spk[:, sl], in0=spk[:, sl], scalar1=-5.0,
                        scalar2=None, op0=ALU.mult), deps=[t_a, t_s])
                    t_m = S.op(engn, lambda e, sl=sl: e.tensor_add(
                        mem[:, sl], mem[:, sl], spk[:, sl]), deps=[t_s, t_m])
                t_s = S.op(engn, lambda e, sl=sl: e.tensor_scalar(
                    out=spk[:, sl], in0=mem[:, sl], scalar1=1.0, scalar2=0.2,
                    op0=ALU.is_gt, op1=ALU.mult), deps=[t_m])
                t_a = S.op(engn, lambda e, sl=sl, accb=accb: e.tensor_add(
                    accb[:, sl], accb[:, sl], spk[:, sl]), deps=[t_s])
                if step < 5:
                    if engn == "vector":
                        t_m = S.op(engn, lambda e, sl=sl: e.scalar_tensor_tensor(
                            out=mem[:, sl], in0=mem[:, sl], scalar=0.9,
                            in1=yf[:, sl], op0=ALU.mult, op1=ALU.add),
                            deps=[t_s, t_m])
                    else:
                        t_m = S.op(engn, lambda e, sl=sl: e.tensor_scalar(
                            out=mem[:, sl], in0=mem[:, sl], scalar1=0.9,
                            scalar2=None, op0=ALU.mult), deps=[t_s, t_m])
                        t_m = S.op(engn, lambda e, sl=sl: e.tensor_add(
                            mem[:, sl], mem[:, sl], yf[:, sl]), deps=[t_m])
            t_fin.append(t_a)
        btok["yf"] = t_fin
        btok["mem"] = t_fin
        btok["spk"] = t_fin

        outd = d_out.rearrange("p (c t) -> p c t", c=DC)[:, :, tile0:tile0 + T]
        t_od = S.dma("gp", lambda e, outd=outd, accb=accb: e.dma_start(
            out=outd, in_=accb[:].rearrange("p (c t) -> p c t", c=DC)),
            deps=t_fin)
        t_out_dmas.append(t_od)
        btok["acc"][it % 2] = t_od
    S.op("sync", lambda e: e.nop(), deps=[t_out_dmas[-1]])

    S.assign()
    streams = {e: [] for e in ENGINES}
    for rec in S.ops:
        streams[rec["eng"]].append(rec)

    with contextlib.ExitStack() as ectx:
        sems = {s: ectx.enter_context(nc.semaphore(s)) for s in SEMS}
        block = ectx.enter_context(nc.Block())

        def emit(ename):
            def body(eng):
                if ename == "sync":
                    eng.nop().then_inc(sems["tick"], 1)
                wm = {}
                pending = False
                for rec in streams[ename]:
                    emitted = False
                    for (s, v) in rec["waits"]:
                        if v > wm.get(s, 0):
                            eng.wait_ge(sems[s], v)
                            wm[s] = v
                            emitted = True
                    if rec["inc"] and pending and not emitted:
                        eng.wait_ge(sems["tick"], 1)
                        emitted = True
                    if emitted:
                        pending = False
                    inst = rec["fn"](eng)
                    if rec["inc"]:
                        inst.then_inc(sems[rec["sem"]], rec["inc"])
                        if rec["idx"] in S.waited:
                            pending = True
            return body

        block.sync(emit("sync"))
        block.gpsimd(emit("gpsimd"))
        block.vector(emit("vector"))
        block.scalar(emit("scalar"))
        block.tensor(emit("tensor"))
    ctxA.close()
    ctx.close()
    return nc


_NC_CACHE = [None]


def _swz(w):
    w = np.ascontiguousarray(w, dtype=np.float32)
    P, F = w.shape
    c = P // 128
    return np.ascontiguousarray(w.reshape(c, 128, F).transpose(1, 0, 2).reshape(128, c * F))


def kernel(L0_attn_x, L0_attn_W_in, L0_attn_conv_w, L0_attn_conv_b, L0_attn_W_x,
           L0_attn_W_dt, L0_attn_b_dt, L0_attn_A_log, L0_attn_D, L0_attn_W_out,
           L0_ffn_W_fc, L0_ffn_W_proj):
    x = np.asarray(L0_attn_x, np.float32)
    shared = {
        "WinT": _swz(np.asarray(L0_attn_W_in, np.float32).T),
        "WoutT": _swz(np.asarray(L0_attn_W_out, np.float32).T),
        "Wxd": _swz(np.concatenate([np.asarray(L0_attn_W_x, np.float32).T,
                                    np.asarray(L0_attn_W_dt, np.float32).T], axis=1)),
        "WfcT": _swz(np.asarray(L0_ffn_W_fc, np.float32).T),
        "WprojT": _swz(np.asarray(L0_ffn_W_proj, np.float32).T),
        "convw": _swz(np.asarray(L0_attn_conv_w, np.float32).reshape(E, 4)),
        "convb": _swz(np.asarray(L0_attn_conv_b, np.float32).reshape(E, 1)),
        "dvec": _swz(np.asarray(L0_attn_D, np.float32).reshape(E, 1)),
        "bdt": np.ascontiguousarray(np.asarray(L0_attn_b_dt, np.float32).reshape(16, 1)),
        "alog": np.ascontiguousarray(np.asarray(L0_attn_A_log, np.float32).reshape(16, 1)),
        "identf": np.eye(128, dtype=np.float32),
        "identb": np.eye(128).astype(BF),
        "tri": np.triu(np.ones((128, 128), dtype=np.float32)),
        "onesb": np.ones((128, 1)).astype(BF),
        "mneg": np.tril(np.full((128, 128), -1e7, np.float32), -1).astype(BF),
    }
    if _NC_CACHE[0] is None:
        _NC_CACHE[0] = build_nc()
    nc = _NC_CACHE[0]
    in_maps = []
    for b in range(8):
        m = dict(shared)
        m["xT"] = _swz(x[b].T)
        in_maps.append(m)
    res = run_bass_kernel_spmd(nc, in_maps, core_ids=list(range(8)))
    outs = []
    for b in range(8):
        o = res.results[b]["outT"]
        outs.append(o.reshape(128, DC, L).transpose(1, 0, 2).reshape(D, L).T)
    return np.ascontiguousarray(np.stack(outs), dtype=np.float32)




# version 38 (final)
# speedup vs baseline: 1.0196x; 1.0196x over previous
import numpy as np
import ml_dtypes
import contextlib
import concourse.bass as bass
import concourse.mybir as mybir
from concourse.bass_utils import run_bass_kernel_spmd

F32 = mybir.dt.float32
BF16 = mybir.dt.bfloat16
AF = mybir.ActivationFunctionType
ALU = mybir.AluOpType
BF = ml_dtypes.bfloat16

D = 1024
E = 2048
N = 16
FF = 4096
L = 4096
T = 256
NT = L // T
Q = 128
DC = D // 128
EC = E // 128
FC = FF // 128

ENGINES = ("sync", "gpsimd", "vector", "scalar", "tensor")
NPOOL = 8
DPOOL = {"sy": ["d_sy%d" % i for i in range(NPOOL)],
         "gp": ["d_gp%d" % i for i in range(NPOOL)]}
BARSEMS = tuple("b%d_%s" % (k, e) for k in range(4) for e in ("sy", "gp", "ve", "sc", "te"))
SEMS = ("s_sy", "s_gp", "s_ve", "s_sc", "s_te", "tick") + tuple(DPOOL["sy"]) \
    + tuple(DPOOL["gp"]) + BARSEMS
ESEM = {"sync": "s_sy", "gpsimd": "s_gp", "vector": "s_ve",
        "scalar": "s_sc", "tensor": "s_te"}


class Sch:

    def __init__(self):
        self.ops = []
        self.dma_ct = {"sy": 0, "gp": 0}

    def op(self, eng, fn, deps=(), force_sem=None):
        rec = {"eng": eng, "fn": fn, "dma": False, "idx": len(self.ops),
               "force_sem": force_sem,
               "deps": [d for d in deps if d is not None]}
        self.ops.append(rec)
        return rec

    def dma(self, queue, fn, deps=()):
        eng = "sync" if queue == "sy" else "gpsimd"
        sem = DPOOL[queue][self.dma_ct[queue] % NPOOL]
        self.dma_ct[queue] += 1
        rec = {"eng": eng, "fn": fn, "dma": True, "q": sem,
               "idx": len(self.ops), "deps": [d for d in deps if d is not None]}
        self.ops.append(rec)
        return rec

    def assign(self):
        self.waited = waited = set()
        for rec in self.ops:
            for d in rec["deps"]:
                if d["dma"] or d["eng"] != rec["eng"]:
                    waited.add(d["idx"])
        cum = {s: 0 for s in SEMS}
        for rec in self.ops:
            if rec["dma"]:
                rec["sem"] = rec["q"]
                rec["inc"] = 16
            elif rec.get("force_sem"):
                rec["sem"] = rec["force_sem"]
                rec["inc"] = 1
            else:
                rec["sem"] = ESEM[rec["eng"]]
                rec["inc"] = 1 if rec["idx"] in waited else 0
            cum[rec["sem"]] += rec["inc"]
            rec["val"] = cum[rec["sem"]]
        run = {s: 0 for s in SEMS}
        for rec in self.ops:
            waits = {}
            for d in rec["deps"]:
                if not d["dma"] and d["eng"] == rec["eng"]:
                    continue
                s = d["sem"]
                v = run[s] if d["dma"] else d["val"]
                assert d["inc"] > 0
                if waits.get(s, 0) < v:
                    waits[s] = v
            rec["waits"] = sorted(waits.items())
            if rec["dma"]:
                run[rec["sem"]] += rec["inc"]
        return cum


ENG2SHORT = {"sync": "sy", "gpsimd": "gp", "vector": "ve", "scalar": "sc",
             "tensor": "te"}


def all_barrier(S, k, extra_deps=None):
    extra_deps = extra_deps or {}
    b1 = {e: S.op(e, lambda eng: eng.nop(), deps=extra_deps.get(e, []),
                  force_sem="b%d_%s" % (k, ENG2SHORT[e])) for e in ENGINES}
    b2 = {e: S.op(e, lambda eng: eng.nop(),
                  deps=[b1[x] for x in ENGINES if x != e],
                  force_sem="b%d_%s" % (k + 1, ENG2SHORT[e])) for e in ENGINES}
    return b1, b2


def build_nc():
    nc = bass.Bass("TRN2", target_bir_lowering=False,
                   detect_race_conditions=False)

    d_x = nc.dram_tensor("xT", [128, DC * L], F32, kind="ExternalInput").ap()
    d_win = nc.dram_tensor("WinT", [128, DC * 4096], F32, kind="ExternalInput").ap()
    d_wout = nc.dram_tensor("WoutT", [128, EC * 1024], F32, kind="ExternalInput").ap()
    d_wxd = nc.dram_tensor("Wxd", [128, EC * 48], F32, kind="ExternalInput").ap()
    d_wfc = nc.dram_tensor("WfcT", [128, DC * 4096], F32, kind="ExternalInput").ap()
    d_wproj = nc.dram_tensor("WprojT", [128, FC * 1024], F32, kind="ExternalInput").ap()
    d_convw = nc.dram_tensor("convw", [128, EC * 4], F32, kind="ExternalInput").ap()
    d_convb = nc.dram_tensor("convb", [128, EC], F32, kind="ExternalInput").ap()
    d_dvec = nc.dram_tensor("dvec", [128, EC], F32, kind="ExternalInput").ap()
    d_bdt = nc.dram_tensor("bdt", [16, 1], F32, kind="ExternalInput").ap()
    d_alog = nc.dram_tensor("alog", [16, 1], F32, kind="ExternalInput").ap()
    d_identf = nc.dram_tensor("identf", [128, 128], F32, kind="ExternalInput").ap()
    d_identb = nc.dram_tensor("identb", [128, 128], BF16, kind="ExternalInput").ap()
    d_tri = nc.dram_tensor("tri", [128, 128], F32, kind="ExternalInput").ap()
    d_onesb = nc.dram_tensor("onesb", [128, 1], BF16, kind="ExternalInput").ap()
    d_mneg = nc.dram_tensor("mneg", [128, 128], BF16, kind="ExternalInput").ap()

    d_out = nc.dram_tensor("outT", [128, DC * L], F32, kind="ExternalOutput").ap()
    d_xn = nc.dram_tensor("xn_hbm", [128, DC * L], BF16, kind="Internal").ap()
    d_attn = nc.dram_tensor("attn_hbm", [128, DC * L], BF16, kind="Internal").ap()

    S = Sch()
    ctx = contextlib.ExitStack()
    A = ctx.enter_context

    identf = A(nc.sbuf_tensor("identf_s", [128, 128], F32))
    identb = A(nc.sbuf_tensor("identb_s", [128, 128], BF16))
    onesb = A(nc.sbuf_tensor("onesb_s", [128, 1], BF16))
    mneg = A(nc.sbuf_tensor("mneg_s", [128, 128], BF16))
    convw = A(nc.sbuf_tensor("convw_s", [128, EC * 4], F32))
    convb = A(nc.sbuf_tensor("convb_s", [128, EC], F32))
    dvec = A(nc.sbuf_tensor("dvec_s", [128, EC], F32))
    bdt = A(nc.sbuf_tensor("bdt_s", [16, 1], F32))
    acol = A(nc.sbuf_tensor("acol_s", [16, 1], F32))
    zer16 = A(nc.sbuf_tensor("zer16_s", [16, 128], F32))

    PB = [A(nc.psum_tensor(f"pb{j}", [128, 512], F32)) for j in range(8)]
    bank_tok = [None] * 8
    rot_mm = [0]
    rot_tr = [0]
    rot_y = [0]
    TRB = (2, 3, 4, 5)

    def take(rot, banks):
        b = banks[rot[0] % len(banks)]
        rot[0] += 1
        return b

    ctxA = contextlib.ExitStack()
    AA = ctxA.enter_context
    win = AA(nc.sbuf_tensor("win_s", [128, DC * 4096], BF16))
    wout = AA(nc.sbuf_tensor("wout_s", [128, EC * 1024], BF16))
    wxd = AA(nc.sbuf_tensor("wxd_s", [128, EC * 48], BF16))
    xt = AA(nc.sbuf_tensor("xt_s", [128, DC * T], F32))
    xs2 = AA(nc.sbuf_tensor("xs2_s", [128, DC * 512], BF16))
    xn = [AA(nc.sbuf_tensor(f"xn{p}", [128, DC * T], BF16)) for p in range(2)]
    u = [AA(nc.sbuf_tensor(f"u{p}", [128, EC * T], BF16)) for p in range(2)]
    tri_unused = None
    ut = AA(nc.sbuf_tensor("ut_s", [128, 2 * E], BF16))
    sz = AA(nc.sbuf_tensor("sz_s", [128, EC * T], BF16))
    y1 = AA(nc.sbuf_tensor("y1_s", [128, EC * T], BF16))
    xib = AA(nc.sbuf_tensor("xib_s", [128, 4 * (T + 3)], BF16))
    xtl = AA(nc.sbuf_tensor("xtl_s", [128, EC * 3], BF16))
    cacc = AA(nc.sbuf_tensor("cacc_s", [128, 4 * T], BF16))
    attn_st = AA(nc.sbuf_tensor("attn_st", [128, DC * T], BF16))
    rep = AA(nc.sbuf_tensor("rep_s", [128, 4 * 128], F32))
    smx = AA(nc.sbuf_tensor("smx_s", [1, 768], F32))

    mct = AA(nc.sbuf_tensor("mct_s", [128, 4], F32))
    NSM = 12
    sma = [AA(nc.sbuf_tensor(f"sma{j}", [16, 128], F32)) for j in range(NSM)]
    bcs = AA(nc.sbuf_tensor("bcs_s", [16, 512], F32))
    _p2a = [AA(nc.sbuf_tensor(f"p2a{q}", [16, 256], F32)) for q in range(2)]
    pack2a = [_p2a, _p2a]
    pack2b = [AA(nc.sbuf_tensor(f"p2b{q}", [16, 256], F32)) for q in range(2)]

    packt4 = [[AA(nc.sbuf_tensor(f"packt{p}{q}", [128, 64], F32))
               for q in range(2)] for p in range(2)]
    dbtn4 = [[AA(nc.sbuf_tensor(f"dbtn{p}{q}", [128, 16], F32))
              for q in range(2)] for p in range(2)]
    epl = [AA(nc.sbuf_tensor(f"epl{p}", [128, 512], BF16)) for p in range(2)]
    macc = AA(nc.sbuf_tensor("macc", [128, 128], F32))
    macc2 = AA(nc.sbuf_tensor("macc2", [128, 128], F32))
    mmb2 = [AA(nc.sbuf_tensor(f"mmb{q}", [128, 128], BF16)) for q in range(2)]
    hst = AA(nc.sbuf_tensor("hst", [16, E], F32))
    hbf = AA(nc.sbuf_tensor("hbf", [16, E], BF16))
    ccum4 = [[AA(nc.sbuf_tensor(f"ccum{p}{q}", [16, 128], BF16))
              for q in range(2)] for p in range(2)]
    ebs2 = [AA(nc.sbuf_tensor(f"ebs{p}", [16, 128], BF16)) for p in range(2)]
    bsc4 = [[AA(nc.sbuf_tensor(f"bsc{p}{q}", [128, 16], BF16))
             for q in range(2)] for p in range(2)]
    dec4 = [[AA(nc.sbuf_tensor(f"dec{p}{q}", [16, 1], F32))
             for q in range(2)] for p in range(2)]

    t_consts = S.dma("sy", lambda e: e.dma_start(out=identf[:], in_=d_identf))
    t_cb = S.dma("sy", lambda e: e.dma_start(out=identb[:], in_=d_identb))
    t_tri = None
    t_ones = S.dma("sy", lambda e: e.dma_start(out=onesb[:], in_=d_onesb))
    t_cw = S.dma("sy", lambda e: e.dma_start(out=convw[:], in_=d_convw))
    t_cbi = S.dma("sy", lambda e: e.dma_start(out=convb[:], in_=d_convb))
    t_dv = S.dma("sy", lambda e: e.dma_start(out=dvec[:], in_=d_dvec))
    t_bdt = S.dma("sy", lambda e: e.dma_start(out=bdt[:], in_=d_bdt))
    t_alog = S.dma("sy", lambda e: e.dma_start(out=acol[:], in_=d_alog))
    t_win = [S.dma("gp", lambda e, c=c: e.dma_start(
        out=win[:, c * 8192:(c + 1) * 8192], in_=d_win[:, c * 8192:(c + 1) * 8192]))
        for c in range(4)]
    t_wout = [S.dma("gp", lambda e, c=c: e.dma_start(
        out=wout[:, c * 8192:(c + 1) * 8192], in_=d_wout[:, c * 8192:(c + 1) * 8192]))
        for c in range(2)]
    t_wxd = S.dma("gp", lambda e: e.dma_start(out=wxd[:], in_=d_wxd))
    t_ac1 = S.op("scalar", lambda e: e.activation(out=acol[:], in_=acol[:], func=AF.Exp),
                 deps=[t_alog])
    t_ac2 = S.op("vector", lambda e: e.tensor_scalar(
        out=acol[:], in0=acol[:], scalar1=-1.0, scalar2=None, op0=ALU.mult),
        deps=[t_ac1])
    t_z16 = S.op("gpsimd", lambda e: e.memset(zer16[:], 0.0))
    t_h0 = S.op("gpsimd", lambda e: e.memset(hst[:], 0.0))
    t_hb0 = S.op("gpsimd", lambda e: e.memset(hbf[:], 0.0))
    t_xtl0 = S.op("gpsimd", lambda e: e.memset(xtl[:], 0.0))
    t_zz0 = S.dma("sy", lambda e: e.dma_start(out=mneg[:], in_=d_mneg))

    tok = {
        "xt_free": None, "xn_free": [None, None], "u_free": [None, None],
        "sz_free": None, "y1_free": None, "attn_free": None,
        "bcs_dt": None, "bcs_b": None, "bcs_c": None,
        "hb": t_hb0, "h": t_h0, "xtl": [t_xtl0] * EC,
        "xib": [None] * 4, "cacc": [None] * 4, "sm": None, "rep": None,
        "pack": [[None] * 2 for _ in range(2)],
        "packb": [None, None],
        "packt_evac": [[None] * 2 for _ in range(2)],
        "epl": [None, None], "macc": None, "mmb": [None, None],
        "small": [None] * NSM, "ccum": [[None] * 2 for _ in range(2)],
        "ebs": [None, None],
        "bsc": [[None] * 2 for _ in range(2)],
        "dec": [[None] * 2 for _ in range(2)], "xs2": None, "mct": None,
        "mc2": None, "dbtn": [[None] * 2 for _ in range(2)],
        "ut_free2": [None, None],
    }

    sb1, sb2 = all_barrier(S, 0, {
        "sync": [t_consts, t_cb, t_ones, t_cw, t_cbi, t_dv, t_bdt,
                 t_alog, t_zz0],
        "gpsimd": t_win + t_wout + [t_wxd],
        "vector": [t_ac2],
    })
    wAll = list(sb2.values())
    del t_tri

    def vclamp(*xs):
        out = []
        for x in xs:
            if x is None:
                continue
            if isinstance(x, list):
                out += [y for y in x if y is not None]
            else:
                out.append(x)
        return out

    fe_state = {}
    spp_state = {}
    spd = {}
    spd2 = {}

    def emit_FE_pre(it):
        par = it % 2
        tile0 = it * T

        xin = d_x.rearrange("p (c t) -> p c t", c=DC)[:, :, tile0:tile0 + T]
        t_xt = S.dma("sy", lambda e, xin=xin: e.dma_start(
            out=xt[:].rearrange("p (c t) -> p c t", c=DC), in_=xin),
            deps=vclamp(tok["xt_free"]))

        xs2v = xs2[:].rearrange("p (c t) -> p c t", c=DC)
        w0 = wAll if it == 0 else []
        t_cp = S.op("vector", lambda e: e.tensor_copy(
            xs2v[:, :, 0:T], xt[:].rearrange("p (c t) -> p c t", c=DC)),
            deps=[t_xt] + vclamp(tok["xs2"]) + w0)
        t_sq = S.op("scalar", lambda e: e.activation(
            out=xs2v[:, :, T:512], in_=xt[:].rearrange("p (c t) -> p c t", c=DC),
            func=AF.Square), deps=[t_xt] + vclamp(tok["xs2"]) + w0)

        fe_state[it] = dict(t_cp=t_cp, t_sq=t_sq, w0=w0)

    def emit_FE_stats(it):
        par = it % 2
        tile0 = it * T
        t_cp = fe_state[it]["t_cp"]
        t_sq = fe_state[it]["t_sq"]
        w0 = fe_state[it]["w0"]

        bs = take(rot_mm, (0, 1))
        dep0 = vclamp(bank_tok[bs]) + [t_cp, t_sq] + w0
        t_stat = None
        for k in range(DC):
            t_stat = S.op("tensor", lambda e, k=k, bs=bs: e.matmul(
                PB[bs][0:1, 0:512], onesb[:], xs2[:, k * 512:(k + 1) * 512],
                start=(k == 0), stop=(k == DC - 1)),
                deps=(dep0 if k == 0 else []))
        tok["xs2"] = t_stat

        t_m = S.op("vector", lambda e, bs=bs: e.tensor_scalar(
            out=smx[0:1, 0:256], in0=PB[bs][0:1, 0:T], scalar1=1.0 / D,
            scalar2=None,
            op0=ALU.mult), deps=[t_stat] + vclamp(tok["sm"]))
        t_e2 = S.op("vector", lambda e, bs=bs: e.tensor_scalar(
            out=smx[0:1, 256:512], in0=PB[bs][0:1, T:512], scalar1=1.0 / D,
            scalar2=1e-5, op0=ALU.mult, op1=ALU.add), deps=[t_stat])
        bank_tok[bs] = t_e2
        t_msq = S.op("vector", lambda e: e.tensor_mul(
            smx[0:1, 512:768], smx[0:1, 0:256], smx[0:1, 0:256]),
            deps=[t_m])
        t_vr = S.op("vector", lambda e: e.tensor_sub(
            smx[0:1, 256:512], smx[0:1, 256:512], smx[0:1, 512:768]),
            deps=[t_e2, t_msq])
        t_ln = S.op("scalar", lambda e: e.activation(
            out=smx[0:1, 256:512], in_=smx[0:1, 256:512], func=AF.Ln),
            deps=[t_vr])
        t_rstd = S.op("scalar", lambda e: e.activation(
            out=smx[0:1, 512:768], in_=smx[0:1, 256:512], func=AF.Exp,
            scale=-0.5),
            deps=[t_ln])
        t_c = S.op("vector", lambda e: e.scalar_tensor_tensor(
            out=smx[0:1, 0:256], in0=smx[0:1, 0:256], scalar=-1.0,
            in1=smx[0:1, 512:768], op0=ALU.mult, op1=ALU.mult), deps=[t_rstd])

        bt = take(rot_tr, TRB)
        t_mcT = []
        pieces = [(512, 0), (0, 1), (640, 2), (128, 3)]
        for sc, dc_ in pieces:
            t_mcT.append(S.op("tensor", lambda e, sc=sc, dc_=dc_, bt=bt:
                              e.transpose(
                PB[bt][:, dc_:dc_ + 1], smx[0:1, sc:sc + 128],
                identf[0:1, 0:1]),
                deps=vclamp(bank_tok[bt]) + [t_rstd, t_c]))
        tok["sm"] = t_mcT
        t_mcE = S.op("scalar", lambda e, bt=bt: e.copy(mct[:], PB[bt][:, 0:4]),
                     deps=t_mcT + (tok["mct"] or []))
        bank_tok[bt] = t_mcE

        t_repE = []
        rep_tts = []
        for gstart in (0, 2):
            bt = take(rot_tr, TRB)
            tts = []
            for j in (gstart, gstart + 1):
                tts.append(S.op("tensor", lambda e, j=j, g=gstart, bt=bt: e.transpose(
                    PB[bt][:, (j - g) * 128:(j - g + 1) * 128],
                    mct[:, j:j + 1].broadcast_to((128, 128)), identf[:]),
                    deps=vclamp(bank_tok[bt]) + [t_mcE]))
            rep_tts += tts
            t_repE.append(S.op("scalar", lambda e, g=gstart, bt=bt: e.copy(
                rep[:, g * 128:(g + 2) * 128], PB[bt][:, 0:256]),
                deps=tts + vclamp(tok["rep"])))
            bank_tok[bt] = t_repE[-1]
        tok["mct"] = rep_tts
        fe_state[it].update(t_repE=t_repE)

    def emit_FE_apply(it):
        par = it % 2
        tile0 = it * T
        t_repE = fe_state[it]["t_repE"]
        xtv = xt[:].rearrange("p (c t) -> p c t", c=DC)
        xnv = xn[par][:].rearrange("p (c t) -> p c t", c=DC)
        repv = rep[:].rearrange("p (g t) -> p g t", g=4)
        t_apply = []
        for h in range(2):
            rsl = repv[:, 2 * h:2 * h + 1, :].broadcast_to((128, DC, 128))
            csl = repv[:, 2 * h + 1:2 * h + 2, :].broadcast_to((128, DC, 128))
            t_a1 = S.op("vector", lambda e, h=h, rsl=rsl, xnv=xnv: e.tensor_mul(
                xnv[:, :, h * 128:(h + 1) * 128],
                xtv[:, :, h * 128:(h + 1) * 128], rsl),
                deps=[t_repE[h]] + vclamp(tok["xn_free"][par]))
            t_a2 = S.op("vector", lambda e, h=h, csl=csl, xnv=xnv: e.tensor_add(
                xnv[:, :, h * 128:(h + 1) * 128],
                xnv[:, :, h * 128:(h + 1) * 128], csl),
                deps=[t_a1])
            t_apply.append(t_a2)
        tok["rep"] = t_apply[-1]
        tok["xt_free"] = t_apply[-1]

        xnd = d_xn.rearrange("p (c t) -> p c t", c=DC)[:, :, tile0:tile0 + T]
        t_xnw = S.dma("gp", lambda e, xnd=xnd, par=par: e.dma_start(
            out=xnd, in_=xn[par][:].rearrange("p (c t) -> p c t", c=DC)),
            deps=t_apply)

        fe_state[it].update(t_apply=t_apply, t_xnw=t_xnw)

    def emit_FE_xz(it):
        par = it % 2
        t_apply = fe_state[it]["t_apply"]
        t_u = [None] * EC
        for m in range(EC):
            bs = take(rot_mm, (0, 1))
            t_mm = None
            mmdep = vclamp(bank_tok[bs]) + t_apply
            for k in range(DC):
                t_mm = S.op("tensor", lambda e, m=m, k=k, bs=bs, par=par: e.matmul(
                    PB[bs][:, 0:T],
                    win[:, k * 4096 + m * 128:k * 4096 + (m + 1) * 128],
                    xn[par][:, k * T:(k + 1) * T],
                    start=(k == 0), stop=(k == DC - 1)),
                    deps=(mmdep if k == 0 else []))
            c = m
            sl = (c % 4) * (T + 3)
            sa = (c % 4) * T
            t_tr = S.op("scalar", lambda e, c=c, sl=sl: e.copy(
                xib[:, sl:sl + 3], xtl[:, c * 3:(c + 1) * 3]),
                deps=vclamp(tok["xib"][c % 4], tok["xtl"][c]))
            t_ev = S.op("scalar", lambda e, bs=bs, sl=sl: e.copy(
                xib[:, sl + 3:sl + T + 3], PB[bs][:, 0:T]),
                deps=[t_mm] + vclamp(tok["xib"][c % 4]))
            bank_tok[bs] = t_ev
            t_ts = S.op("scalar", lambda e, c=c, sl=sl: e.copy(
                xtl[:, c * 3:(c + 1) * 3], xib[:, sl + T:sl + T + 3]),
                deps=[t_ev, t_tr])
            tok["xtl"][c] = t_ts
            t_c0 = S.op("vector", lambda e, c=c, sl=sl, sa=sa: e.tensor_scalar(
                out=cacc[:, sa:sa + T], in0=xib[:, sl:sl + T],
                scalar1=convw[:, 4 * c:4 * c + 1],
                scalar2=convb[:, c:c + 1], op0=ALU.mult, op1=ALU.add),
                deps=[t_ev, t_tr] + vclamp(tok["cacc"][c % 4]))
            t_ck = t_c0
            for k in range(1, 4):
                t_ck = S.op("vector", lambda e, c=c, k=k, sl=sl, sa=sa:
                            e.scalar_tensor_tensor(
                    out=cacc[:, sa:sa + T], in0=xib[:, sl + k:sl + k + T],
                    scalar=convw[:, 4 * c + k:4 * c + k + 1],
                    in1=cacc[:, sa:sa + T],
                    op0=ALU.mult, op1=ALU.add), deps=[t_ck])
            tok["xib"][c % 4] = t_ck
            if c % 2 == 1:
                t_up = S.op("scalar", lambda e, c=c, sa=sa, par=par: e.activation(
                    out=u[par][:, (c - 1) * T:(c + 1) * T],
                    in_=cacc[:, sa - T:sa + T], func=AF.Silu),
                    deps=[t_ck, prev_ck] + vclamp(tok["u_free"][par]))
                t_u[c - 1] = t_up
                t_u[c] = t_up
                tok["cacc"][(c - 1) % 4] = t_up
            prev_ck = t_ck
            if c % 2 == 1:
                tok["cacc"][c % 4] = t_u[c]
        fe_state[it].update(t_u=t_u)

    def emit_SPP(it):
        par = it % 2
        tp = it % 2
        t_u = fe_state[it]["t_u"]

        bwB = take(rot_tr, TRB)
        t_g = None
        for k in range(EC):
            kdep = [t_u[k]] + (vclamp(bank_tok[bwB]) if k == 0 else [])
            t_g = S.op("tensor", lambda e, k=k, bwB=bwB, par=par: e.matmul(
                PB[bwB][0:16, 0:T],
                wxd[:, k * 48 + 32:k * 48 + 48],
                u[par][:, k * T:(k + 1) * T],
                start=(k == 0), stop=(k == EC - 1)),
                deps=kdep)
        t_gdt = t_g
        bwA = take(rot_tr, TRB)
        tgs = []
        for j, col in ((0, 0), (1, 256)):
            t_g = None
            for k in range(EC):
                kdep = [t_u[k]] + (vclamp(bank_tok[bwA])
                                   if (j == 0 and k == 0) else [])
                t_g = S.op("tensor", lambda e, j=j, k=k, bwA=bwA, col=col,
                           par=par: e.matmul(
                    PB[bwA][0:16, col:col + T],
                    wxd[:, k * 48 + j * 16:k * 48 + (j + 1) * 16],
                    u[par][:, k * T:(k + 1) * T],
                    start=(k == 0), stop=(k == EC - 1)),
                    deps=kdep)
            tgs.append(t_g)

        bc_readers = []
        dt_readers = []
        front = [None, None]
        for q in range(2):
            qs = q * Q
            o = 6 * q
            sm = sma[o:o + 6]
            t_d1 = S.op("scalar", lambda e, qs=qs, sm=sm, bwB=bwB:
                        e.activation(
                out=sm[0][:], in_=PB[bwB][0:16, qs:qs + Q], func=AF.Abs,
                bias=bdt[:]), deps=[t_gdt] + vclamp(tok["small"][o + 0]))
            t_d2 = S.op("scalar", lambda e, sm=sm: e.activation(
                out=sm[0][:], in_=sm[0][:], func=AF.Exp, scale=-1.0), deps=[t_d1])
            t_d3 = S.op("scalar", lambda e, sm=sm: e.activation(
                out=sm[0][:], in_=sm[0][:], func=AF.Ln, bias=1.0), deps=[t_d2])
            t_d4 = S.op("scalar", lambda e, qs=qs, sm=sm, bwB=bwB:
                        e.activation(
                out=sm[1][:], in_=PB[bwB][0:16, qs:qs + Q], func=AF.Relu,
                bias=bdt[:]), deps=[t_gdt] + vclamp(tok["small"][o + 1]))
            t_delta = S.op("vector", lambda e, sm=sm: e.tensor_add(
                sm[2][:], sm[0][:], sm[1][:]),
                deps=[t_d3, t_d4] + vclamp(tok["small"][o + 2]))
            tok["small"][o + 0] = t_delta
            tok["small"][o + 1] = t_delta
            t_dal = S.op("vector", lambda e, sm=sm: e.tensor_scalar(
                out=sm[3][:], in0=sm[2][:], scalar1=acol[:], scalar2=None,
                op0=ALU.mult), deps=[t_delta] + vclamp(tok["small"][o + 3]))
            t_S = S.op("vector", lambda e, sm=sm: e.tensor_tensor_scan(
                out=sm[4][:], data0=sm[3][:], data1=zer16[:], initial=0.0,
                op0=ALU.add, op1=ALU.add),
                deps=[t_dal] + vclamp(tok["small"][o + 4]))
            tok["small"][o + 3] = t_S
            front[q] = dict(t_delta=t_delta, t_S=t_S, t_d1=t_d1, t_d4=t_d4)
        bank_tok[bwB] = [front[0]["t_d1"], front[0]["t_d4"],
                         front[1]["t_d1"], front[1]["t_d4"]]
        for q in range(2):
            o = 6 * q
            sm = sma[o:o + 6]
            fr = front[q]
            t_S = fr["t_S"]
            t_p2 = S.op("vector", lambda e, sm=sm, pb=pack2b[q]: e.tensor_scalar(
                out=pb[:, 0:128], in0=sm[4][:], scalar1=-1.0, scalar2=None,
                op0=ALU.mult), deps=[t_S] + vclamp(tok["packb"][q]))
            t_eS = S.op("scalar", lambda e, sm=sm: e.activation(
                out=sm[1][:], in_=sm[4][:], func=AF.Exp),
                deps=[t_S] + vclamp(tok["small"][o + 1]))
            t_a2s = S.op("vector", lambda e, sm=sm: e.tensor_scalar(
                out=sm[3][:], in0=sm[4][:], scalar1=-1.0,
                scalar2=sm[4][:, 127:128], op0=ALU.mult, op1=ALU.add),
                deps=[t_S] + vclamp(tok["small"][o + 3]))
            t_e2s = S.op("scalar", lambda e, sm=sm: e.activation(
                out=sm[3][:], in_=sm[3][:], func=AF.Exp), deps=[t_a2s])
            t_dec = S.op("scalar", lambda e, sm=sm, dec=dec4[tp][q]:
                         e.activation(
                out=dec[:], in_=sm[4][:, 127:128], func=AF.Exp),
                deps=[t_S] + vclamp(tok["dec"][tp][q]))
            fr.update(t_p2=t_p2, t_eS=t_eS, t_e2s=t_e2s, t_dec=t_dec)

        t_bc0 = S.op("scalar", lambda e, bwA=bwA: e.copy(
            bcs[0:16, 0:T], PB[bwA][0:16, 0:T]),
            deps=[tgs[0], tgs[1]] + vclamp(tok["bcs_b"]))
        t_bc1 = S.op("scalar", lambda e, bwA=bwA: e.copy(
            bcs[0:16, T:2 * T], PB[bwA][0:16, T:2 * T]),
            deps=[tgs[1], tgs[0]] + vclamp(tok["bcs_c"]))
        bank_tok[bwA] = t_bc1

        back = [None, None]
        for q in range(2):
            qs = q * Q
            o = 6 * q
            sm = sma[o:o + 6]
            fr = front[q]
            t_S = fr["t_S"]
            t_dB = S.op("vector", lambda e, qs=qs, sm=sm, pb=pack2b[q]:
                        e.tensor_mul(
                pb[:, 128:256], sm[2][:], bcs[0:16, qs:qs + Q]),
                deps=[t_bc0, fr["t_delta"]] + vclamp(tok["packb"][q]))
            tok["small"][o + 2] = t_dB
            bc_readers.append(t_dB)
            t_cp = S.op("vector", lambda e, qs=qs, sm=sm: e.tensor_scalar(
                out=sm[5][:], in0=bcs[0:16, 256 + qs:256 + qs + Q],
                scalar1=1e-38, scalar2=None, op0=ALU.max),
                deps=[t_bc1] + vclamp(tok["small"][o + 5]))
            t_lcp = S.op("scalar", lambda e, sm=sm: e.activation(
                out=sm[5][:], in_=sm[5][:], func=AF.Ln), deps=[t_cp])
            t_p0 = S.op("vector", lambda e, sm=sm, pa=pack2a[tp][q]:
                        e.tensor_add(
                pa[:, 0:128], sm[4][:], sm[5][:]),
                deps=[t_lcp, t_S] + vclamp(tok["pack"][tp][q]))
            t_cm = S.op("vector", lambda e, qs=qs, sm=sm: e.tensor_scalar(
                out=sm[0][:], in0=bcs[0:16, 256 + qs:256 + qs + Q],
                scalar1=-1.0, scalar2=1e-38, op0=ALU.mult, op1=ALU.max),
                deps=[t_bc1, fr["t_delta"]] + vclamp(tok["small"][o + 0]))
            t_lcm = S.op("scalar", lambda e, sm=sm: e.activation(
                out=sm[0][:], in_=sm[0][:], func=AF.Ln), deps=[t_cm])
            t_p1 = S.op("vector", lambda e, sm=sm, pa=pack2a[tp][q]:
                        e.tensor_add(
                pa[:, 128:256], sm[4][:], sm[0][:]), deps=[t_lcm])
            bc_readers += [t_cp, t_cm]
            t_ccum = S.op("vector", lambda e, qs=qs, sm=sm, ccum=ccum4[tp][q]:
                          e.tensor_mul(
                ccum[:], bcs[0:16, 256 + qs:256 + qs + Q], sm[1][:]),
                deps=[fr["t_eS"], t_bc1] + vclamp(tok["ccum"][tp][q]))
            bc_readers.append(t_ccum)
            tok["small"][o + 1] = t_ccum
            t_ebs = S.op("vector", lambda e, sm=sm, pb=pack2b[q],
                         eb=ebs2[q]: e.tensor_mul(
                eb[:], sm[3][:], pb[:, 128:256]),
                deps=[fr["t_e2s"], t_dB] + vclamp(tok["ebs"][q]))
            tok["small"][o + 3] = t_ebs
            tok["small"][o + 0] = t_p1
            tok["small"][o + 4] = [t_p0, t_p1, fr["t_dec"]]
            tok["small"][o + 5] = t_p0
            back[q] = dict(t_dB=t_dB, t_p0=t_p0, t_p1=t_p1, t_ccum=t_ccum,
                           t_ebs=t_ebs)

        tok["bcs_b"] = [back[0]["t_dB"], back[1]["t_dB"]]
        tok["bcs_c"] = bc_readers
        spp_state[it] = dict(front=front, back=back, t_u=t_u)

    def emit_SPP_T(it):
        par = it % 2
        tp = it % 2
        st = spp_state.pop(it)
        front = st["front"]
        back = st["back"]
        t_u = st["t_u"]
        for q in range(2):
            packt = packt4[tp][q]
            dbtn = dbtn4[tp][q]
            fr = front[q]
            bk = back[q]
            t_ebs = bk["t_ebs"]
            bt = take(rot_tr, TRB)
            t_pT = []
            for g in range(4):
                srcap = (pack2a[tp][q][:, (g % 2) * 128:(g % 2) * 128 + 128]
                         if g < 2 else
                         pack2b[q][:, (g - 2) * 128:(g - 2) * 128 + 128])
                t_pT.append(S.op("tensor", lambda e, g=g, bt=bt, srcap=srcap:
                            e.transpose(
                    PB[bt][:, g * 16:(g + 1) * 16], srcap,
                    identf[0:16, 0:16]),
                    deps=vclamp(bank_tok[bt]) +
                         [bk["t_dB"], bk["t_p0"], bk["t_p1"], fr["t_p2"]]))
            t_pE = S.op("scalar", lambda e, bt=bt, packt=packt: e.copy(
                packt[:], PB[bt][:, 0:64]),
                deps=t_pT + vclamp(tok["packt_evac"][tp][q]))
            bank_tok[bt] = t_pE
            t_dbtn = S.op("vector", lambda e, packt=packt, dbtn=dbtn:
                          e.tensor_scalar(
                out=dbtn[:], in0=packt[:, 48:64], scalar1=-1.0, scalar2=None,
                op0=ALU.mult), deps=[t_pE] + vclamp(tok["dbtn"][tp][q]))
            bt = take(rot_tr, TRB)
            t_bT = S.op("tensor", lambda e, bt=bt, eb=ebs2[q]: e.transpose(
                PB[bt][:].bitcast(BF16)[:, 0:16], eb[:], identb[0:16, 0:16]),
                deps=vclamp(bank_tok[bt]) + [t_ebs])
            t_bE = S.op("scalar", lambda e, bt=bt, bsc=bsc4[tp][q]: e.copy(
                bsc[:], PB[bt][:].bitcast(BF16)[:, 0:16]),
                deps=[t_bT] + vclamp(tok["bsc"][tp][q]))
            bank_tok[bt] = t_bE
            tok["ebs"][q] = t_bT
            tok["packb"][q] = [t_pT[2], t_pT[3], t_ebs]
            tok["pack"][tp][q] = [t_pT[0], t_pT[1]]
            spd[(it, q)] = dict(t_pE=t_pE, t_dbtn=t_dbtn, t_ccum=bk["t_ccum"],
                                t_bE=t_bE, t_dec=fr["t_dec"],
                                t_rows=[bk["t_p0"], bk["t_p1"]], t_u=t_u)


    def emit_PLANES1(it):
        par = it % 2
        tp = it % 2
        y_evacs = []
        for q in range(2):
            qs = q * Q
            packt = packt4[tp][q]
            dbtn = dbtn4[tp][q]
            ccum = ccum4[tp][q]
            bsc = bsc4[tp][q]
            dec = dec4[tp][q]
            st = spd.pop((it, q))
            t_pE = st["t_pE"]
            t_dbtn = st["t_dbtn"]
            t_ccum = st["t_ccum"]
            t_bE = st["t_bE"]
            t_dec = st["t_dec"]
            t_rows = st["t_rows"]
            t_u = st["t_u"]
            spd2[(it, q)] = st
            ut_evacs = []
            for g4 in range(4):
                bt = take(rot_tr, TRB)
                tts = []
                for j in range(4):
                    c = g4 * 4 + j
                    tts.append(S.op("tensor", lambda e, c=c, j=j, bt=bt, qs=qs,
                                    par=par: e.transpose(
                        PB[bt][:].bitcast(BF16)[:, j * 128:(j + 1) * 128],
                        u[par][:, c * T + qs:c * T + qs + Q], identb[:]),
                        deps=vclamp(bank_tok[bt]) + [t_u[c]]))
                if g4 % 2 == 0:
                    ut_evacs.append(S.op("scalar", lambda e, g4=g4, q=q, bt=bt:
                                         e.copy(
                        ut[:, q * E + g4 * 512:q * E + (g4 + 1) * 512],
                        PB[bt][:].bitcast(BF16)[:, 0:512]),
                        deps=tts + vclamp(tok["ut_free2"][q])))
                else:
                    ut_evacs.append(S.op("scalar", lambda e, g4=g4, q=q, bt=bt:
                                         e.copy(
                        ut[:, q * E + g4 * 512:q * E + (g4 + 1) * 512],
                        PB[bt][:].bitcast(BF16)[:, 0:512]),
                        deps=tts + vclamp(tok["ut_free2"][q])))
                bank_tok[bt] = ut_evacs[-1]
            spd2[(it, q)]["ut_evacs"] = ut_evacs
            t_comb = None
            for grp in range(8):
                gbase = grp * 4
                bt = take(rot_tr, TRB)
                rrp = grp % 2
                tts = []
                for j in range(4):
                    g = gbase + j
                    col = (g // 16) * 16 + (g % 16)
                    tts.append(S.op("tensor", lambda e, col=col, j=j, bt=bt,
                                    packt=packt: e.matmul(
                        PB[bt][:, j * 128:(j + 1) * 128],
                        packt[:, col:col + 1].broadcast_to((128, 128)),
                        identf[:], is_transpose=True,
                        start=(j == 0), stop=False),
                        deps=vclamp(bank_tok[bt]) + [t_pE]))
                tts.append(S.op("tensor", lambda e, bt=bt: e.matmul(
                    PB[bt][:, 0:512], identb[:],
                    mneg[:].unsqueeze(1).broadcast_to((128, 4, 128)),
                    start=False, stop=True),
                    deps=[]))
                t_exs = []
                for j in range(4):
                    g = gbase + j
                    n = g % 16
                    t_exs.append(S.op("scalar", lambda e, j=j, n=n, rrp=rrp,
                                      bt=bt, packt=packt: e.activation(
                        out=epl[rrp][:, j * 128:(j + 1) * 128],
                        in_=PB[bt][:, j * 128:(j + 1) * 128], func=AF.Exp,
                        bias=packt[:, 32 + n:33 + n]),
                        deps=[tts[-1], t_pE] + vclamp(tok["epl"][rrp])))
                bank_tok[bt] = t_exs[-1]
                ma = macc if gbase < 16 else macc2
                for j in range(4):
                    g = gbase + j
                    n = g % 16
                    sc = (dbtn if g >= 16 else packt)
                    scol = (n if g >= 16 else 48 + n)
                    first = (g == 0 or g == 16)
                    scdep = t_dbtn if g >= 16 else t_pE
                    if first:
                        t_comb = S.op("vector", lambda e, j=j, sc=sc, scol=scol,
                                      rrp=rrp, ma=ma: e.tensor_scalar(
                            out=ma[:], in0=epl[rrp][:, j * 128:(j + 1) * 128],
                            scalar1=sc[:, scol:scol + 1], scalar2=None,
                            op0=ALU.mult),
                            deps=[t_exs[j], scdep] + vclamp(tok["macc"]))
                    else:
                        t_comb = S.op("vector", lambda e, j=j, sc=sc, scol=scol,
                                      rrp=rrp, ma=ma: e.scalar_tensor_tensor(
                            out=ma[:], in0=epl[rrp][:, j * 128:(j + 1) * 128],
                            scalar=sc[:, scol:scol + 1], in1=ma[:],
                            op0=ALU.mult, op1=ALU.add),
                            deps=[t_exs[j], t_comb, scdep])
                    if g == 15:
                        t_comb_a = t_comb
                tok["epl"][rrp] = t_comb
            tok["dbtn"][tp][q] = t_comb
            tok["packt_evac"][tp][q] = t_comb
            t_mm2 = S.op("vector", lambda e, mmb=mmb2[q]: e.tensor_add(
                mmb[:], macc[:], macc2[:]),
                deps=[t_comb_a, t_comb] + vclamp(tok["mmb"][q]))
            tok["macc"] = t_mm2
            spd2[(it, q)]["t_mm2"] = t_mm2
    def emit_PLANES2(it):
        par = it % 2
        tp = it % 2
        y_evacs = []
        for q in range(2):
            qs = q * Q
            ccum = ccum4[tp][q]
            bsc = bsc4[tp][q]
            dec = dec4[tp][q]
            st = spd2.pop((it, q))
            t_ccum = st["t_ccum"]
            t_bE = st["t_bE"]
            t_dec = st["t_dec"]
            t_u = st["t_u"]
            t_mm2 = st["t_mm2"]
            ut_evacs = st["ut_evacs"]
            for g4 in range(4):
                by = take(rot_y, (6, 7))
                t_ymm = []
                for j in range(4):
                    c = g4 * 4 + j
                    t_i1 = S.op("tensor", lambda e, c=c, j=j, q=q, by=by: e.matmul(
                        PB[by][:, j * 128:(j + 1) * 128],
                        ut[:, q * E + c * 128:q * E + (c + 1) * 128], mmb2[q][:],
                        start=True, stop=False),
                        deps=vclamp(bank_tok[by]) + [t_mm2, ut_evacs[c // 4]])
                    t_i2 = S.op("tensor", lambda e, c=c, j=j, by=by, ccum=ccum:
                                e.matmul(
                        PB[by][:, j * 128:(j + 1) * 128],
                        hbf[0:16, c * 128:(c + 1) * 128], ccum[:],
                        start=False, stop=True),
                        deps=[t_ccum, tok["hb"]])
                    t_ymm.append(t_i2)
                t_ye = None
                for j in range(4):
                    c = g4 * 4 + j
                    t_ye = S.op("vector", lambda e, c=c, j=j, q=q, by=by, qs=qs,
                                par=par: e.scalar_tensor_tensor(
                        out=y1[:, c * T + qs:c * T + qs + Q],
                        in0=u[par][:, c * T + qs:c * T + qs + Q],
                        scalar=dvec[:, c:c + 1], in1=PB[by][:, j * 128:(j + 1) * 128],
                        op0=ALU.mult, op1=ALU.add),
                        deps=[t_ymm[-1], t_u[c]] + vclamp(tok["y1_free"]))
                    y_evacs.append(t_ye)
                bank_tok[by] = t_ye
            tok["mmb"][q] = y_evacs[-1]
            tok["ccum"][tp][q] = y_evacs[-1]
            t_hup = []
            for i in range(4):
                bs = take(rot_mm, (0, 1))
                t_hc = S.op("tensor", lambda e, i=i, q=q, bs=bs, bsc=bsc: e.matmul(
                    PB[bs][0:16, 0:512], bsc[:],
                    ut[:, q * E + i * 512:q * E + (i + 1) * 512],
                    start=True, stop=True),
                    deps=vclamp(bank_tok[bs]) + [t_bE, ut_evacs[i]])
                t_h = S.op("vector", lambda e, i=i, bs=bs, dec=dec:
                           e.scalar_tensor_tensor(
                    out=hst[0:16, i * 512:(i + 1) * 512],
                    in0=hst[0:16, i * 512:(i + 1) * 512], scalar=dec[:],
                    in1=PB[bs][0:16, 0:512], op0=ALU.mult, op1=ALU.add),
                    deps=[t_hc, t_dec, tok["h"]] +
                         ([tok["hb"]] if tok["hb"] is not None else []))
                bank_tok[bs] = t_h
                t_hup.append(t_h)
            tok["bsc"][tp][q] = t_hup[-1]
            tok["dec"][tp][q] = t_hup[-1]
            tok["h"] = t_hup[-1]
            t_hb = S.op("scalar", lambda e: e.copy(hbf[:], hst[:]),
                        deps=t_hup)
            tok["hb"] = t_hb
            tok["ut_free2"][q] = t_hb


        fe_state[it]["y_evacs"] = y_evacs
        tok["u_free"][par] = y_evacs[-1]

    def emit_TAIL_Z(it):
        par = it % 2
        tile0 = it * T
        t_apply = fe_state[it]["t_apply"]

        t_sz = [None] * EC
        for c in range(EC):
            m = EC + c
            bs = take(rot_mm, (0, 1))
            t_mm = None
            mmdep = vclamp(bank_tok[bs]) + t_apply
            for k in range(DC):
                t_mm = S.op("tensor", lambda e, m=m, k=k, bs=bs, par=par: e.matmul(
                    PB[bs][:, 0:T],
                    win[:, k * 4096 + m * 128:k * 4096 + (m + 1) * 128],
                    xn[par][:, k * T:(k + 1) * T],
                    start=(k == 0), stop=(k == DC - 1)),
                    deps=(mmdep if k == 0 else []))
            t_sz[c] = S.op("scalar", lambda e, c=c, bs=bs: e.activation(
                out=sz[:, c * T:(c + 1) * T], in_=PB[bs][:, 0:T], func=AF.Silu),
                deps=[t_mm] + vclamp(tok["sz_free"]))
            bank_tok[bs] = t_sz[c]

        fe_state[it]["t_sz"] = t_sz

    def emit_TAIL_R(it):
        par = it % 2
        tile0 = it * T
        t_sz = fe_state[it]["t_sz"]
        y_evacs = fe_state[it]["y_evacs"]
        t_xnw = fe_state[it]["t_xnw"]

        t_gk = [None] * EC
        for k in range(EC):
            t_gk[k] = S.op("vector", lambda e, k=k: e.tensor_mul(
                y1[:, k * T:(k + 1) * T], y1[:, k * T:(k + 1) * T],
                sz[:, k * T:(k + 1) * T]),
                deps=[y_evacs[k], y_evacs[16 + k], t_sz[k]])
        t_gate = t_gk[-1]
        tok["sz_free"] = list(t_gk)
        tok["u_free"][0] = t_gate
        tok["u_free"][1] = t_gate

        t_attn_dma_dep = []
        for m in range(DC):
            bs = take(rot_mm, (0, 1))
            t_mo = None
            modep = vclamp(bank_tok[bs])
            for k in range(EC):
                kdep = list(modep if k == 0 else [])
                if m == 0:
                    kdep.append(t_gk[k])
                t_mo = S.op("tensor", lambda e, m=m, k=k, bs=bs: e.matmul(
                    PB[bs][:, 0:T],
                    wout[:, k * 1024 + m * 128:k * 1024 + (m + 1) * 128],
                    y1[:, k * T:(k + 1) * T],
                    start=(k == 0), stop=(k == EC - 1)),
                    deps=kdep)
            t_ae = S.op("scalar", lambda e, m=m, bs=bs: e.copy(
                attn_st[:, m * T:(m + 1) * T], PB[bs][:, 0:T]),
                deps=[t_mo] + vclamp(tok["attn_free"]))
            bank_tok[bs] = t_ae
            t_attn_dma_dep.append(t_ae)
        attnd = d_attn.rearrange("p (c t) -> p c t", c=DC)[:, :, tile0:tile0 + T]
        t_attn_dma = S.dma("gp", lambda e, attnd=attnd: e.dma_start(
            out=attnd, in_=attn_st[:].rearrange("p (c t) -> p c t", c=DC)),
            deps=t_attn_dma_dep)
        tok["attn_free"] = t_attn_dma
        tok["xn_free"][par] = t_sz[-1]
        tok["y1_free"] = t_attn_dma_dep[-1]

    emit_FE_pre(0)
    emit_FE_stats(0)
    emit_FE_apply(0)
    emit_FE_xz(0)
    emit_SPP(0)
    emit_SPP_T(0)
    for it in range(NT):
        if it + 1 < NT:
            emit_FE_pre(it + 1)
            emit_FE_stats(it + 1)
            emit_FE_apply(it + 1)
        emit_PLANES1(it)
        if it + 1 < NT:
            emit_FE_xz(it + 1)
        emit_TAIL_Z(it)
        if it + 1 < NT:
            emit_SPP(it + 1)
        emit_PLANES2(it)
        emit_TAIL_R(it)
        if it + 1 < NT:
            emit_SPP_T(it + 1)

    barrier1, barrier2 = all_barrier(
        S, 2, {"sync": vclamp(tok["attn_free"], tok["xn_free"][0],
                              tok["xn_free"][1])})

    ctxA.close()
    AB = ctxA.enter_context
    wfc = AB(nc.sbuf_tensor("wfc_s", [128, DC * 4096], BF16))
    wproj = AB(nc.sbuf_tensor("wproj_s", [128, FC * 1024], BF16))
    xnb = AB(nc.sbuf_tensor("xnb_s", [128, DC * T], BF16))
    xb2 = AB(nc.sbuf_tensor("xb2_s", [128, DC * T], F32))
    atb = AB(nc.sbuf_tensor("atb_s", [128, DC * T], BF16))
    hsq = AB(nc.sbuf_tensor("hsq_s", [128, FC * T], BF16))
    hrel = AB(nc.sbuf_tensor("hrel_s", [128, T], BF16))
    yf = AB(nc.sbuf_tensor("yf_s", [128, DC * T], F32))
    mem = AB(nc.sbuf_tensor("mem_s", [128, DC * T], F32))
    accb2 = [AB(nc.sbuf_tensor(f"accb{p}", [128, DC * T], F32)) for p in range(2)]
    spk = AB(nc.sbuf_tensor("spk_s", [128, DC * T], F32))

    bdeps = list(barrier2.values())
    t_wfc = [S.dma("gp", lambda e, c=c: e.dma_start(
        out=wfc[:, c * 8192:(c + 1) * 8192], in_=d_wfc[:, c * 8192:(c + 1) * 8192]),
        deps=bdeps) for c in range(4)]
    t_wproj = [S.dma("gp", lambda e, c=c: e.dma_start(
        out=wproj[:, c * 8192:(c + 1) * 8192],
        in_=d_wproj[:, c * 8192:(c + 1) * 8192]), deps=bdeps) for c in range(4)]
    wB = t_wfc + t_wproj

    btok = {"xn": bdeps, "x": bdeps, "attn": bdeps, "hsq": None,
            "hrel": [None, None], "yf": [], "mem": [], "acc": [None, None],
            "spk": [], "out_dma": None}
    t_out_dmas = []
    for it in range(NT):
        tile0 = it * T
        xnd = d_xn.rearrange("p (c t) -> p c t", c=DC)[:, :, tile0:tile0 + T]
        t_xn = S.dma("sy", lambda e, xnd=xnd: e.dma_start(
            out=xnb[:].rearrange("p (c t) -> p c t", c=DC), in_=xnd),
            deps=btok["xn"] if isinstance(btok["xn"], list) else [btok["xn"]])
        xind = d_x.rearrange("p (c t) -> p c t", c=DC)[:, :, tile0:tile0 + T]
        t_x = S.dma("sy", lambda e, xind=xind: e.dma_start(
            out=xb2[:].rearrange("p (c t) -> p c t", c=DC), in_=xind),
            deps=btok["x"] if isinstance(btok["x"], list) else [btok["x"]])
        attnd = d_attn.rearrange("p (c t) -> p c t", c=DC)[:, :, tile0:tile0 + T]
        t_at = S.dma("sy", lambda e, attnd=attnd: e.dma_start(
            out=atb[:].rearrange("p (c t) -> p c t", c=DC), in_=attnd),
            deps=btok["attn"] if isinstance(btok["attn"], list) else [btok["attn"]])
        accb = accb2[it % 2]
        t_acc0 = S.op("vector", lambda e, accb=accb: e.tensor_add(
            accb[:], xb2[:], atb[:]),
            deps=[t_x, t_at] + vclamp(btok["acc"][it % 2]))
        btok["x"] = t_acc0
        btok["attn"] = t_acc0

        t_h = [None] * FC
        for m in range(FC):
            bs = take(rot_mm, (0, 1))
            t_mm = None
            mmdep = vclamp(bank_tok[bs]) + [t_xn]
            for k in range(DC):
                kdep = list(mmdep if k == 0 else [])
                if it == 0:
                    kdep.append(t_wfc[k // 2])
                t_mm = S.op("tensor", lambda e, m=m, k=k, bs=bs: e.matmul(
                    PB[bs][:, 0:T],
                    wfc[:, k * 4096 + m * 128:k * 4096 + (m + 1) * 128],
                    xnb[:, k * T:(k + 1) * T],
                    start=(k == 0), stop=(k == DC - 1)),
                    deps=kdep)
            sh = 0
            t_r = S.op("scalar", lambda e, bs=bs, sh=sh: e.activation(
                out=hrel[:, sh:sh + T], in_=PB[bs][:, 0:T], func=AF.Relu),
                deps=[t_mm] + vclamp(btok["hrel"][m % 2]))
            bank_tok[bs] = t_r
            t_h[m] = S.op("scalar", lambda e, m=m, sh=sh: e.activation(
                out=hsq[:, m * T:(m + 1) * T], in_=hrel[:, sh:sh + T],
                func=AF.Square), deps=[t_r] + vclamp(btok["hsq"]))
            btok["hrel"][m % 2] = t_h[m]
        btok["xn"] = t_h[-1]

        t_yf = []
        for m in range(DC):
            bs = take(rot_mm, (0, 1))
            t_mm = None
            for k in range(FC):
                kdep = [t_h[k]] + (vclamp(bank_tok[bs]) if k == 0 else [])
                if it == 0:
                    kdep.append(t_wproj[k // 8])
                t_mm = S.op("tensor", lambda e, m=m, k=k, bs=bs: e.matmul(
                    PB[bs][:, 0:T],
                    wproj[:, k * 1024 + m * 128:k * 1024 + (m + 1) * 128],
                    hsq[:, k * T:(k + 1) * T],
                    start=(k == 0), stop=(k == FC - 1)),
                    deps=kdep)
            t_yf.append(S.op("scalar", lambda e, m=m, bs=bs: e.copy(
                yf[:, m * T:(m + 1) * T], PB[bs][:, 0:T]),
                deps=[t_mm] + (btok["yf"] or [])))
            bank_tok[bs] = t_yf[-1]
        btok["hsq"] = t_yf[-1]

        yfdep = t_yf
        t_fin = []
        VS = DC * T
        for engn, c0, c1 in (("vector", 0, VS),):
            sl = slice(c0, c1)
            t_s = S.op(engn, lambda e, sl=sl: e.tensor_scalar(
                out=spk[:, sl], in0=yf[:, sl], scalar1=1.0, scalar2=0.2,
                op0=ALU.is_gt, op1=ALU.mult),
                deps=yfdep + (btok["spk"] or []))
            t_a = S.op(engn, lambda e, sl=sl, accb=accb: e.tensor_add(
                accb[:, sl], accb[:, sl], spk[:, sl]), deps=[t_s, t_acc0])
            t_m = S.op(engn, lambda e, sl=sl: e.tensor_scalar(
                out=mem[:, sl], in0=yf[:, sl], scalar1=1.9, scalar2=None,
                op0=ALU.mult), deps=yfdep + (btok["mem"] or []))
            for step in range(2, 6):
                if engn == "vector":
                    t_m = S.op(engn, lambda e, sl=sl: e.scalar_tensor_tensor(
                        out=mem[:, sl], in0=spk[:, sl], scalar=-5.0,
                        in1=mem[:, sl], op0=ALU.mult, op1=ALU.add),
                        deps=[t_a, t_m])
                else:
                    t_s = S.op(engn, lambda e, sl=sl: e.tensor_scalar(
                        out=spk[:, sl], in0=spk[:, sl], scalar1=-5.0,
                        scalar2=None, op0=ALU.mult), deps=[t_a, t_s])
                    t_m = S.op(engn, lambda e, sl=sl: e.tensor_add(
                        mem[:, sl], mem[:, sl], spk[:, sl]), deps=[t_s, t_m])
                t_s = S.op(engn, lambda e, sl=sl: e.tensor_scalar(
                    out=spk[:, sl], in0=mem[:, sl], scalar1=1.0, scalar2=0.2,
                    op0=ALU.is_gt, op1=ALU.mult), deps=[t_m])
                t_a = S.op(engn, lambda e, sl=sl, accb=accb: e.tensor_add(
                    accb[:, sl], accb[:, sl], spk[:, sl]), deps=[t_s])
                if step < 5:
                    if engn == "vector":
                        t_m = S.op(engn, lambda e, sl=sl: e.scalar_tensor_tensor(
                            out=mem[:, sl], in0=mem[:, sl], scalar=0.9,
                            in1=yf[:, sl], op0=ALU.mult, op1=ALU.add),
                            deps=[t_s, t_m])
                    else:
                        t_m = S.op(engn, lambda e, sl=sl: e.tensor_scalar(
                            out=mem[:, sl], in0=mem[:, sl], scalar1=0.9,
                            scalar2=None, op0=ALU.mult), deps=[t_s, t_m])
                        t_m = S.op(engn, lambda e, sl=sl: e.tensor_add(
                            mem[:, sl], mem[:, sl], yf[:, sl]), deps=[t_m])
            t_fin.append(t_a)
        btok["yf"] = t_fin
        btok["mem"] = t_fin
        btok["spk"] = t_fin

        outd = d_out.rearrange("p (c t) -> p c t", c=DC)[:, :, tile0:tile0 + T]
        t_od = S.dma("gp", lambda e, outd=outd, accb=accb: e.dma_start(
            out=outd, in_=accb[:].rearrange("p (c t) -> p c t", c=DC)),
            deps=t_fin)
        t_out_dmas.append(t_od)
        btok["acc"][it % 2] = t_od
    S.op("sync", lambda e: e.nop(), deps=[t_out_dmas[-1]])

    S.assign()
    streams = {e: [] for e in ENGINES}
    for rec in S.ops:
        streams[rec["eng"]].append(rec)

    with contextlib.ExitStack() as ectx:
        sems = {s: ectx.enter_context(nc.semaphore(s)) for s in SEMS}
        block = ectx.enter_context(nc.Block())

        def emit(ename):
            def body(eng):
                if ename == "sync":
                    eng.nop().then_inc(sems["tick"], 1)
                wm = {}
                pending = False
                for rec in streams[ename]:
                    emitted = False
                    for (s, v) in rec["waits"]:
                        if v > wm.get(s, 0):
                            eng.wait_ge(sems[s], v)
                            wm[s] = v
                            emitted = True
                    if rec["inc"] and pending and not emitted:
                        eng.wait_ge(sems["tick"], 1)
                        emitted = True
                    if emitted:
                        pending = False
                    inst = rec["fn"](eng)
                    if rec["inc"]:
                        inst.then_inc(sems[rec["sem"]], rec["inc"])
                        if rec["idx"] in S.waited:
                            pending = True
            return body

        block.sync(emit("sync"))
        block.gpsimd(emit("gpsimd"))
        block.vector(emit("vector"))
        block.scalar(emit("scalar"))
        block.tensor(emit("tensor"))
    ctxA.close()
    ctx.close()
    return nc


_NC_CACHE = [None]


def _swz(w):
    w = np.ascontiguousarray(w, dtype=np.float32)
    P, F = w.shape
    c = P // 128
    return np.ascontiguousarray(w.reshape(c, 128, F).transpose(1, 0, 2).reshape(128, c * F))


def kernel(L0_attn_x, L0_attn_W_in, L0_attn_conv_w, L0_attn_conv_b, L0_attn_W_x,
           L0_attn_W_dt, L0_attn_b_dt, L0_attn_A_log, L0_attn_D, L0_attn_W_out,
           L0_ffn_W_fc, L0_ffn_W_proj):
    x = np.asarray(L0_attn_x, np.float32)
    shared = {
        "WinT": _swz(np.asarray(L0_attn_W_in, np.float32).T),
        "WoutT": _swz(np.asarray(L0_attn_W_out, np.float32).T),
        "Wxd": _swz(np.concatenate([np.asarray(L0_attn_W_x, np.float32).T,
                                    np.asarray(L0_attn_W_dt, np.float32).T], axis=1)),
        "WfcT": _swz(np.asarray(L0_ffn_W_fc, np.float32).T),
        "WprojT": _swz(np.asarray(L0_ffn_W_proj, np.float32).T),
        "convw": _swz(np.asarray(L0_attn_conv_w, np.float32).reshape(E, 4)),
        "convb": _swz(np.asarray(L0_attn_conv_b, np.float32).reshape(E, 1)),
        "dvec": _swz(np.asarray(L0_attn_D, np.float32).reshape(E, 1)),
        "bdt": np.ascontiguousarray(np.asarray(L0_attn_b_dt, np.float32).reshape(16, 1)),
        "alog": np.ascontiguousarray(np.asarray(L0_attn_A_log, np.float32).reshape(16, 1)),
        "identf": np.eye(128, dtype=np.float32),
        "identb": np.eye(128).astype(BF),
        "tri": np.triu(np.ones((128, 128), dtype=np.float32)),
        "onesb": np.ones((128, 1)).astype(BF),
        "mneg": np.tril(np.full((128, 128), -1e7, np.float32), -1).astype(BF),
    }
    if _NC_CACHE[0] is None:
        _NC_CACHE[0] = build_nc()
    nc = _NC_CACHE[0]
    in_maps = []
    for b in range(8):
        m = dict(shared)
        m["xT"] = _swz(x[b].T)
        in_maps.append(m)
    res = run_bass_kernel_spmd(nc, in_maps, core_ids=list(range(8)))
    outs = []
    for b in range(8):
        o = res.results[b]["outT"]
        outs.append(o.reshape(128, DC, L).transpose(1, 0, 2).reshape(D, L).T)
    return np.ascontiguousarray(np.stack(outs), dtype=np.float32)


